# version 13
# baseline (speedup 1.0000x reference)
import math
from contextlib import ExitStack
import numpy as np
import concourse.bass as bass
from concourse.bass import broadcast_tensor_aps
import concourse.mybir as mybir
from concourse.bass_utils import run_bass_kernel_spmd

F32 = mybir.dt.float32
BF16 = mybir.dt.bfloat16
AF = mybir.ActivationFunctionType
ALU = mybir.AluOpType

NCORE = 8
D = 1024
LSEQ = 16384
TC = LSEQ // NCORE
NH, HD, NPAT = 16, 64, 32
NSTATE = 4096
DIN = 2048
DLOC = DIN // NCORE
SST = 16
RNK = 64
KCV = 4
NL = 4
EPS = 1e-5
NK_D = D // 128
CH = 512
NCH = TC // CH
SCH = 1024
NSC = LSEQ // SCH
DIRS = ("f", "b")

PHASES = ("attn", "ffn1", "mamba", "ffn2")
DBG = False
NL_RUN = NL
MSTEPS = ("A", "B", "C")
CPARTS = 9


class Res:
    __slots__ = ("writer", "readers", "name")

    def __init__(self, name=""):
        self.writer = None
        self.readers = []
        self.name = name


class Prog:
    ENGS = ("q", "g", "v", "s", "t")

    def __init__(self):
        self.ops = {e: [] for e in self.ENGS}
        self.count = {e: 0 for e in self.ENGS}
        self.fence_deps = {}

    def fence(self):
        self.fence_deps = {e: v for e, v in self.count.items()
                           if v and not e.startswith("_")}

    NDQ = 16

    def emit(self, eng, fn, reads=(), writes=(), deps=(), dma=False,
             sem_key=None):
        dd = {}

        def add(tok):
            if tok is None:
                return
            e, v = tok
            dd[e] = max(dd.get(e, 0), v)

        for e, v in self.fence_deps.items():
            dd[e] = max(dd.get(e, 0), v)
        for tok in deps:
            add(tok)
        for r in reads:
            add(r.writer)
        for w in writes:
            add(w.writer)
            for tok in w.readers:
                add(tok)
        if dma:
            di = self.count.get("_dmai", 0)
            self.count["_dmai"] = di + 1
            key = f"dq{di % self.NDQ}"
            prev = self.count.get(key, 0)
            if prev:
                add((key, prev))
            inc = 16
        else:
            key = sem_key or eng
            inc = 1
        self.count[key] = self.count.get(key, 0) + inc
        tok = (key, self.count[key])
        self.ops[eng].append((tuple(sorted(dd.items())), fn, inc, key))
        for r in reads:
            r.readers.append(tok)
        for w in writes:
            w.writer = tok
            w.readers = []
        return tok

    def build(self, nc):
        keys = sorted(k for k in self.count if not k.startswith("_"))
        with ExitStack() as es:
            sems = {k: es.enter_context(nc.semaphore(f"sem_{k}"))
                    for k in keys}
            with nc.Block() as block:
                def runner(ename):
                    def _(h):
                        water = {k: 0 for k in keys}
                        for deps, fn, inc, key in self.ops[ename]:
                            for e, v in deps:
                                if water.get(e, 0) < v:
                                    h.wait_ge(sems[e], v)
                                    water[e] = v
                            fn(h).then_inc(sems[key], inc)
                    return _

                block.sync(runner("q"))
                block.gpsimd(runner("g"))
                block.vector(runner("v"))
                block.scalar(runner("s"))
                block.tensor(runner("t"))


def bitonic_stages(n=32):
    out = []
    K = 2
    while K <= n:
        j = K // 2
        while j >= 1:
            out.append((K, j))
            j //= 2
        K *= 2
    return out


def _verify_bitonic():
    rng = np.random.default_rng(0)
    x = rng.standard_normal((64, 32))
    a = x.copy()
    for K, j in bitonic_stages(32):
        for i in np.arange(32)[(np.arange(32) & j) == 0]:
            p = i | j
            desc = (i & K) == 0
            hi = np.maximum(a[:, i], a[:, p]).copy()
            lo = np.minimum(a[:, i], a[:, p]).copy()
            a[:, i], a[:, p] = (hi, lo) if desc else (lo, hi)
    assert np.allclose(a, -np.sort(-x, axis=-1))


_verify_bitonic()


def build_program(phases=None):
    if phases is None:
        phases = PHASES
    nc = bass.Bass("TRN2", target_bir_lowering=False)
    P = Prog()
    R = {}

    def res(name):
        if name not in R:
            R[name] = Res(name)
        return R[name]

    ein = {}

    def EI(name, shape):
        ein[name] = nc.dram_tensor(name, list(shape), F32,
                                   kind="ExternalInput").ap()

    EI("x_in", [D, TC])
    out_d = nc.dram_tensor("out", [D, TC], F32, kind="ExternalOutput").ap()
    for w in ("Wk", "Wo"):
        EI(w, [D, D])
    for w in ("WqS", "WkS", "WvS"):
        EI(w, [D, 128])
    for b in ("bqS", "bkS", "bvS"):
        EI(b, [128, 1])
    for b in ("bk", "bo"):
        EI(b, [128, NK_D])
    EI("kpT", [D, NPAT])
    EI("w1", [D, NSTATE]); EI("b1", [128, 32])
    EI("w2", [NSTATE, D]); EI("b2", [128, NK_D])
    EI("mWT", [2 * D, D]); EI("mb", [128, NK_D])
    EI("nfw", [128, NK_D]); EI("nfb", [128, NK_D])
    for dr in DIRS:
        for l in range(NL):
            p = f"m{dr}{l}_"
            EI(p + "nw", [128, NK_D]); EI(p + "nb", [128, NK_D])
            EI(p + "inWuT", [D, DLOC]); EI(p + "inWzT", [D, DLOC])
            EI(p + "cw", [128, 2 * KCV]); EI(p + "cb", [128, 2])
            EI(p + "xpWT", [DLOC, 96])
            EI(p + "dtWT", [RNK, DLOC]); EI(p + "dtb", [128, 2])
            EI(p + "Alog", [128, 2 * SST]); EI(p + "Dp", [128, 2])
            EI(p + "outWT", [DLOC, D])

    dram = {}

    def DI(name, shape, dtype=F32, shared=False):
        dram[name] = nc.dram_tensor(
            name, list(shape), dtype, kind="Internal",
            addr_space="Shared" if shared else "Local").ap()

    DI("h_xres", [D, TC])
    DI("ag_in_at", [D, TC], BF16)
    DI("ag_out_at", [NCORE, D, TC], BF16, shared=True)
    for dr in DIRS:
        DI(f"h_res_{dr}", [D, TC])
        DI(f"h_z_{dr}", [DLOC, LSEQ], BF16)
        DI(f"h_uc_{dr}", [DLOC, LSEQ], BF16)
        DI(f"ag_in_{dr}", [D, TC], BF16)
        DI(f"ag_out_{dr}", [NCORE, D, TC], BF16, shared=True)
        DI(f"ar_in_{dr}", [96, LSEQ], BF16)
        DI(f"ar_out_{dr}", [96, LSEQ], BF16)
        DI(f"rs_in_{dr}", [NCORE, D, TC], BF16)
        DI(f"rs_out_{dr}", [D, TC], BF16)

    stack0 = ExitStack()

    def SBp(name, shape, dtype=F32):
        return stack0.enter_context(nc.sbuf_tensor(name, list(shape), dtype))

    xres = SBp("xres", [128, NK_D, TC])
    wslab = SBp("wslab", [128, 2, 4096], BF16)
    wld = SBp("wld", [128, 4, 512])
    st32 = SBp("st32", [128, 6, CH])
    stt = SBp("stt", [1, 4, CH])
    ones_r = SBp("ones_r", [1, 128])
    ones_rb = SBp("ones_rb", [1, 128], BF16)
    ones_c = SBp("ones_c", [128, 1])
    ident_b = SBp("ident_b", [128, 128], BF16)
    ident_f = SBp("ident_f", [128, 128])
    smalls = SBp("smalls", [128, 96])
    b1sb = SBp("b1sb", [128, 32])
    epsc = SBp("epsc", [128, 1])
    onepp = SBp("onepp", [128, 1])

    psum = [stack0.enter_context(nc.psum_tensor(f"pb{i}", [128, 512], F32))
            for i in range(4)]
    psum45 = stack0.enter_context(nc.psum_tensor("pb45", [128, 1024], F32))
    psum67 = stack0.enter_context(nc.psum_tensor("pb67", [128, 1024], F32))
    psum.extend([psum45[:, 0:512], psum45[:, 512:1024],
                 psum67[:, 0:512], psum67[:, 512:1024]])
    PS = [res(f"pb{i}") for i in range(8)]

    P.emit("v", lambda v: v.memset(ones_r[:], 1.0), writes=[res("ones_r")])
    P.emit("v", lambda v: v.memset(ones_rb[:], 1.0), writes=[res("ones_rb")])
    P.emit("v", lambda v: v.memset(ones_c[:], 1.0), writes=[res("ones_c")])
    P.emit("v", lambda v: v.memset(epsc[:], EPS), writes=[res("epsc")])
    P.emit("v", lambda v: v.memset(onepp[:], 1.0), writes=[res("onepp")])
    P.emit("g", lambda g: g.memset(ident_f[:], 0.0), writes=[res("ident_f")])
    P.emit("g", lambda g: g.affine_select(
        out=ident_f[:], in_=ident_f[:], compare_op=ALU.not_equal, fill=1.0,
        base=0, pattern=[[-1, 128]], channel_multiplier=1),
        writes=[res("ident_f")])
    P.emit("v", lambda v: v.tensor_copy(ident_b[:], ident_f[:]),
           reads=[res("ident_f")], writes=[res("ident_b")])
    P.emit("q", lambda q: q.dma_start(
        out=xres[:], in_=ein["x_in"].rearrange("(k p) t -> p k t", p=128)),
        writes=[res("xres")], dma=True)
    for nm, col in (("bk", 1), ("bo", 3),
                    ("b2", 4), ("mb", 5), ("nfw", 6), ("nfb", 7)):
        P.emit("q", lambda q, nm=nm, col=col: q.dma_start(
            out=smalls[:, 8 * col:8 * col + 8], in_=ein[nm][:]),
            writes=[res("smalls")], dma=True)
    P.emit("q", lambda q: q.dma_start(
        out=b1sb[:], in_=ein["b1"][:]),
        writes=[res("b1sb")], dma=True)

    def smallv(col, k):
        return smalls[:, 8 * col + k:8 * col + k + 1]

    dbg_outs = {}

    def dbg(name, ap, rr):
        if not DBG:
            return
        sh = list(ap.shape)
        d = nc.dram_tensor("dbg_" + name, sh, ap.dtype,
                           kind="ExternalOutput").ap()
        dbg_outs[name] = sh
        P.emit("q", lambda q: q.dma_start(out=d[:], in_=ap),
               reads=[rr], dma=True)

    def mm_stream(W_ap, K_total, M_total, rhs_get, n_chunks, nw, evac,
                  rhs_res=()):
        KT = (K_total + 127) // 128
        MB = M_total // 128
        for m in range(MB):
            sl = m % 2
            wr = res(f"wslab{sl}")
            for k in range(KT):
                kh = min(128, K_total - 128 * k)
                ls = (m * KT + k) % 4
                lr = res(f"wld{ls}")
                P.emit("q", lambda q, m=m, k=k, kh=kh, ls=ls: q.dma_start(
                    out=wld[0:kh, ls, 0:128],
                    in_=W_ap[128 * k:128 * k + kh, 128 * m:128 * m + 128]),
                    writes=[lr], dma=True)
                P.emit("v", lambda v, k=k, kh=kh, sl=sl, ls=ls: v.tensor_copy(
                    wslab[0:kh, sl, 128 * k:128 * k + 128], wld[0:kh, ls, 0:128]),
                    reads=[lr], writes=[wr])
            for n in range(n_chunks):
                bank = n % 4
                for k in range(KT):
                    kh = min(128, K_total - 128 * k)
                    P.emit("t", lambda t, n=n, k=k, kh=kh, sl=sl, bank=bank: t.matmul(
                        psum[bank][:, 0:nw],
                        wslab[0:kh, sl, 128 * k:128 * k + 128],
                        rhs_get(k, n), start=(k == 0), stop=(k == KT - 1)),
                        reads=[wr] + list(rhs_res), writes=[PS[bank]])
                evac(m, n, bank)

    def ln_fm(src_get, dst_get, affine, in_res=(), out_res=(),
              nk=NK_D, n_chunks=NCH, post_chunk=None):
        ir = list(in_res)
        orr = list(out_res)
        for n in range(n_chunks):
            for k in range(nk):
                P.emit("t", lambda t, k=k, n=n: t.matmul(
                    psum[6][0:1, :], ones_c[:], src_get(k, n),
                    start=(k == 0), stop=(k == nk - 1)),
                    reads=[res("ones_c")] + ir, writes=[PS[6]])
            for k in range(nk):
                P.emit("s", lambda s, k=k, n=n: s.activation(
                    out=st32[:, 0, :], in_=src_get(k, n), func=AF.Square),
                    reads=ir, writes=[res("stsl0")])
                P.emit("t", lambda t, k=k: t.matmul(
                    psum[4][0:1, :], ones_c[:], st32[:, 0, :],
                    start=(k == 0), stop=(k == nk - 1)),
                    reads=[res("stsl0"), res("ones_c")], writes=[PS[4]])
            Dn = float(nk * 128)
            c0 = n * CH
            P.emit("v", lambda v, c0=c0: v.tensor_scalar_mul(
                stt[0:1, 0, :], psum[6][0:1, :], 1.0 / Dn),
                reads=[PS[6]], writes=[res("stt")])
            P.emit("v", lambda v, c0=c0: v.tensor_scalar_mul(
                stt[0:1, 1, :], psum[4][0:1, :], 1.0 / Dn),
                reads=[PS[4]], writes=[res("stt")])
            P.emit("v", lambda v, c0=c0: v.tensor_tensor(
                out=stt[0:1, 2, :], in0=stt[0:1, 0, :],
                in1=stt[0:1, 0, :], op=ALU.mult),
                reads=[res("stt")], writes=[res("stt")])
            P.emit("v", lambda v, c0=c0: v.tensor_tensor(
                out=stt[0:1, 1, :], in0=stt[0:1, 1, :],
                in1=stt[0:1, 2, :], op=ALU.subtract),
                reads=[res("stt")], writes=[res("stt")])
            P.emit("s", lambda s, c0=c0: s.activation(
                out=stt[0:1, 2, :], in_=stt[0:1, 1, :],
                func=AF.Sqrt, bias=epsc[0:1, 0:1]),
                reads=[res("stt"), res("epsc")], writes=[res("stt")])
            P.emit("v", lambda v, c0=c0: v.reciprocal(
                out=stt[0:1, 3, :], in_=stt[0:1, 2, :]),
                reads=[res("stt")], writes=[res("stt")])
            P.emit("t", lambda t, c0=c0: t.matmul(
                psum[7][:, :], ones_r[:], stt[0:1, 0, :],
                start=True, stop=True),
                reads=[res("stt"), res("ones_r")], writes=[PS[7]])
            P.emit("t", lambda t, c0=c0: t.matmul(
                psum[5][:, :], ones_r[:], stt[0:1, 3, :],
                start=True, stop=True),
                reads=[res("stt"), res("ones_r")], writes=[PS[5]])
            for k in range(nk):
                P.emit("v", lambda v, k=k, n=n: v.tensor_tensor(
                    out=st32[:, 1, :], in0=src_get(k, n),
                    in1=psum[7][:, :], op=ALU.subtract),
                    reads=[PS[7]] + ir, writes=[res("stsl1")])
                if affine is None:
                    P.emit("v", lambda v, k=k, n=n: v.tensor_tensor(
                        out=dst_get(k, n), in0=st32[:, 1, :],
                        in1=psum[5][:, :], op=ALU.mult),
                        reads=[res("stsl1"), PS[5]], writes=orr)
                else:
                    wg, bg = affine
                    P.emit("v", lambda v, k=k, n=n: v.tensor_tensor(
                        out=st32[:, 2, :], in0=st32[:, 1, :],
                        in1=psum[5][:, :], op=ALU.mult),
                        reads=[res("stsl1"), PS[5]], writes=[res("stsl2")])
                    P.emit("v", lambda v, k=k, n=n: v.tensor_scalar(
                        out=dst_get(k, n), in0=st32[:, 2, :],
                        scalar1=wg(k), scalar2=bg(k), op0=ALU.mult, op1=ALU.add),
                        reads=[res("stsl2")] + ir, writes=orr)
            if post_chunk is not None:
                post_chunk(n)

    def phase_attn(biga, xlnA, spx, kiota, ktb, kp_sb, vsm_loc, ATb,
                   ktloc, rhsb1):
        ln_fm(lambda k, n: xres[:, k, n * CH:(n + 1) * CH],
              lambda k, n: xlnA[:, k, n * CH:(n + 1) * CH], None,
              in_res=[res("xres")], out_res=[res("xlnA")])
        P.emit("g", lambda q: q.dma_start(
            out=dram["ag_in_at"].rearrange("(k p) t -> p k t", p=128),
            in_=xlnA[:, :, :]),
            reads=[res("xlnA")], writes=[res("ag_at_in")], dma=True)
        collective("AllGather", ALU.bypass, dram["ag_in_at"][:],
                   dram["ag_out_at"][:], "ag_at")
        for k in range(NK_D):
            P.emit("q", lambda q, k=k: q.dma_start(
                out=st32[:, 3, 0:NPAT], in_=ein["kpT"][128 * k:128 * k + 128, :]),
                writes=[res("stsl3")], dma=True)
            P.emit("v", lambda v, k=k: v.tensor_copy(
                kp_sb[:, k, :], st32[:, 3, 0:NPAT]),
                reads=[res("stsl3")], writes=[res("kp_sb")])
        def evk(m, n, b):
            P.emit("s", lambda s, m=m, b=b: s.activation(
                out=ktb[:, m, :], in_=psum[b][:, 0:NPAT],
                func=AF.Identity, bias=smallv(1, m)),
                reads=[PS[b], res("smalls")], writes=[res("ktb")])
        mm_stream(ein["Wk"], D, D, lambda k, n: kp_sb[:, k, :], 1, NPAT, evk,
                  rhs_res=[res("kp_sb")])
        for k in range(NK_D):
            P.emit("q", lambda q, k=k: q.dma_start(
                out=wld[:, 0, 0:128],
                in_=ein["WkS"][128 * k:128 * k + 128, :]),
                writes=[res("wld0")], dma=True)
            P.emit("v", lambda v, k=k: v.tensor_copy(
                wslab[:, 0, 128 * k:128 * k + 128], wld[:, 0, 0:128]),
                reads=[res("wld0")], writes=[res("wslab0")])
            P.emit("t", lambda t, k=k: t.matmul(
                psum[4][:, 0:NPAT], wslab[:, 0, 128 * k:128 * k + 128],
                kp_sb[:, k, :], start=(k == 0), stop=(k == NK_D - 1)),
                reads=[res("wslab0"), res("kp_sb")], writes=[PS[4]])
        P.emit("q", lambda q: q.dma_start(
            out=st32[:, 4, 0:1], in_=ein["bkS"][:]),
            writes=[res("stsl4")], dma=True)
        P.emit("s", lambda s: s.activation(
            out=ktloc[:, :], in_=psum[4][:, 0:NPAT], func=AF.Identity,
            bias=st32[:, 4, 0:1]),
            reads=[PS[4], res("stsl4")], writes=[res("ktloc")])
        for k in range(NK_D):
            P.emit("q", lambda q, k=k: q.dma_start(
                out=wld[:, 0, 0:128],
                in_=ein["WvS"][128 * k:128 * k + 128, :]),
                writes=[res("wld0")], dma=True)
            P.emit("v", lambda v, k=k: v.tensor_copy(
                wslab[:, 0, 128 * k:128 * k + 128], wld[:, 0, 0:128]),
                reads=[res("wld0")], writes=[res("wslab0")])
            P.emit("t", lambda t, k=k: t.matmul(
                psum[5][0:NPAT, 0:128], ktb[:, k, :],
                wslab[:, 0, 128 * k:128 * k + 128],
                start=(k == 0), stop=(k == NK_D - 1)),
                reads=[res("wslab0"), res("ktb")], writes=[PS[5]])
        P.emit("v", lambda v: v.tensor_copy(vsm_loc[:, :], psum[5][0:NPAT, 0:128]),
               reads=[PS[5]], writes=[res("vsm")])
        P.emit("q", lambda q: q.dma_start(
            out=st32[:, 5, 0:1], in_=ein["bqS"][:]),
            writes=[res("stsl5")], dma=True)
        P.emit("q", lambda q: q.dma_start(
            out=st32[:, 5, 1:2], in_=ein["bvS"][:]),
            writes=[res("stsl5")], dma=True)
        for k in range(NK_D):
            P.emit("q", lambda q, k=k: q.dma_start(
                out=wld[:, 0, 0:128],
                in_=ein["WqS"][128 * k:128 * k + 128, :]),
                writes=[res("wld0")], dma=True)
            P.emit("v", lambda v, k=k: v.tensor_copy(
                wslab[:, 1, 128 * k:128 * k + 128], wld[:, 0, 0:128]),
                reads=[res("wld0")], writes=[res("wslab1")])
        for n in range(LSEQ // CH):
            slot, toff = divmod(n * CH, TC)
            pp = n % 2
            P.emit("q", lambda q, slot=slot, toff=toff, pp=pp: q.dma_start(
                out=rhsb1[:, pp, :, :],
                in_=dram["ag_out_at"][slot, :, toff:toff + CH].rearrange(
                    "(k p) t -> p k t", p=128)),
                reads=[res("ag_at_out")], writes=[res(f"rhsb1{pp}")], dma=True)
            bank = n % 4
            for k in range(NK_D):
                P.emit("t", lambda t, k=k, bank=bank, pp=pp: t.matmul(
                    psum[bank][:, :], wslab[:, 1, 128 * k:128 * k + 128],
                    rhsb1[:, pp, k, :], start=(k == 0), stop=(k == NK_D - 1)),
                    reads=[res("wslab1"), res(f"rhsb1{pp}")], writes=[PS[bank]])
            P.emit("s", lambda s, n=n, bank=bank: s.activation(
                out=biga[:, n * CH:(n + 1) * CH], in_=psum[bank][:, :],
                func=AF.Identity, bias=st32[:, 5, 0:1]),
                reads=[PS[bank], res("stsl5")], writes=[res("biga")])
        scale = 1.0 / math.sqrt(HD)
        stages = bitonic_stages(NPAT)
        for og in range(LSEQ // 1024):
            p0 = og * 1024
            for ch in range(8):
                for hp in range(2):
                    P.emit("t", lambda t, ch=ch, hp=hp, p0=p0: t.matmul(
                        psum[5][:, (ch * 2 + hp) * 32:(ch * 2 + hp) * 32 + 32],
                        biga[64 * hp:64 * hp + 64,
                             p0 + ch * 128:p0 + ch * 128 + 128],
                        ktloc[64 * hp:64 * hp + 64, :],
                        start=True, stop=True),
                        reads=[res("biga"), res("ktloc")], writes=[PS[5]])
            P.emit("s", lambda s: s.activation(
                out=spx[:, 0, :], in_=psum[5][:, :], func=AF.Identity,
                scale=scale), reads=[PS[5]], writes=[res("spx0")])
            P.emit("v", lambda v: v.tensor_copy(spx[:, 1, :], spx[:, 0, :]),
                   reads=[res("spx0")], writes=[res("spx1")])
            cur, oth = 1, 2
            for (K, j) in stages:
                def rr(buf, K=K, j=j):
                    return spx[:, buf, :].rearrange(
                        "p (g two m b r) -> p g two m b r",
                        two=2, m=K // (2 * j), b=2, r=j)
                for d in (0, 1):
                    desc = (d == 0) or (K == NPAT)
                    mx = 0 if desc else 1
                    P.emit("v", lambda v, cur=cur, oth=oth, d=d, mx=mx, K=K, j=j:
                           v.tensor_tensor(
                               out=rr(oth, K, j)[:, :, d, :, mx, :],
                               in0=rr(cur, K, j)[:, :, d, :, 0, :],
                               in1=rr(cur, K, j)[:, :, d, :, 1, :], op=ALU.max),
                           reads=[res(f"spx{cur}")], writes=[res(f"spx{oth}")])
                    P.emit("v", lambda v, cur=cur, oth=oth, d=d, mx=mx, K=K, j=j:
                           v.tensor_tensor(
                               out=rr(oth, K, j)[:, :, d, :, 1 - mx, :],
                               in0=rr(cur, K, j)[:, :, d, :, 0, :],
                               in1=rr(cur, K, j)[:, :, d, :, 1, :], op=ALU.min),
                           reads=[res(f"spx{cur}")], writes=[res(f"spx{oth}")])
                cur, oth = oth, cur
            srt = cur
            seq = [(srt, 4), (4, 1), (1, 4), (4, 1), (1, 4)]
            for step, dshift in enumerate((1, 2, 4, 8, 16)):
                sbuf_, dbuf = seq[step]
                sv = spx[:, sbuf_, :].rearrange("p (h s) -> p h s", s=NPAT)
                dv = spx[:, dbuf, :].rearrange("p (h s) -> p h s", s=NPAT)
                P.emit("v", lambda v, sv=sv, dv=dv, ds=dshift: v.tensor_tensor(
                    out=dv[:, :, ds:], in0=sv[:, :, ds:],
                    in1=sv[:, :, 0:NPAT - ds], op=ALU.add),
                    reads=[res(f"spx{sbuf_}")], writes=[res(f"spx{dbuf}")])
                P.emit("v", lambda v, sv=sv, dv=dv, ds=dshift: v.tensor_copy(
                    dv[:, :, 0:ds], sv[:, :, 0:ds]),
                    reads=[res(f"spx{sbuf_}")], writes=[res(f"spx{dbuf}")])
            s3 = lambda sl: spx[:, sl, :].rearrange("p (h s) -> p h s", s=NPAT)
            P.emit("v", lambda v, srt=srt: v.tensor_tensor(
                out=spx[:, 1, :], in0=spx[:, srt, :], in1=kiota[:],
                op=ALU.mult),
                reads=[res(f"spx{srt}"), res("kiota")], writes=[res("spx1")])
            P.emit("v", lambda v: v.tensor_scalar_add(
                spx[:, 1, :], spx[:, 1, :], 1.0),
                reads=[res("spx1")], writes=[res("spx1")])
            P.emit("v", lambda v: v.tensor_tensor(
                out=spx[:, 1, :], in0=spx[:, 1, :], in1=spx[:, 4, :],
                op=ALU.is_gt),
                reads=[res("spx1"), res("spx4")], writes=[res("spx1")])
            P.emit("v", lambda v, s3=s3, srt=srt: v.tensor_tensor(
                out=s3(srt)[:, :, 0:NPAT - 1], in0=s3(1)[:, :, 0:NPAT - 1],
                in1=s3(1)[:, :, 1:NPAT], op=ALU.subtract),
                reads=[res("spx1")], writes=[res(f"spx{srt}")])
            P.emit("v", lambda v, s3=s3, srt=srt: v.tensor_copy(
                s3(srt)[:, :, NPAT - 1:NPAT], s3(1)[:, :, NPAT - 1:NPAT]),
                reads=[res("spx1")], writes=[res(f"spx{srt}")])
            P.emit("v", lambda v, srt=srt: v.tensor_tensor(
                out=spx[:, srt, :], in0=spx[:, srt, :], in1=spx[:, 4, :],
                op=ALU.mult),
                reads=[res(f"spx{srt}"), res("spx4")], writes=[res(f"spx{srt}")])
            P.emit("v", lambda v, s3=s3: v.tensor_reduce(
                out=spx[:, 4, 0:NH], in_=s3(1),
                axis=mybir.AxisListType.X, op=ALU.add),
                reads=[res("spx1")], writes=[res("spx4")])
            P.emit("v", lambda v, s3=s3, srt=srt: v.tensor_reduce(
                out=spx[:, 4, 16:16 + NH], in_=s3(srt),
                axis=mybir.AxisListType.X, op=ALU.add),
                reads=[res(f"spx{srt}")], writes=[res("spx4")])
            P.emit("v", lambda v: v.tensor_scalar_add(
                spx[:, 4, 16:16 + NH], spx[:, 4, 16:16 + NH], -1.0),
                reads=[res("spx4")], writes=[res("spx4")])
            P.emit("v", lambda v: v.reciprocal(
                out=spx[:, 4, 32:32 + NH], in_=spx[:, 4, 0:NH]),
                reads=[res("spx4")], writes=[res("spx4")])
            P.emit("v", lambda v: v.tensor_tensor(
                out=spx[:, 4, 32:32 + NH], in0=spx[:, 4, 16:16 + NH],
                in1=spx[:, 4, 32:32 + NH], op=ALU.mult),
                reads=[res("spx4")], writes=[res("spx4")])
            for g2 in range(16):
                P.emit("v", lambda v, g2=g2: v.tensor_scalar(
                    out=spx[:, 2, 32 * g2:32 * g2 + 32],
                    in0=spx[:, 0, 32 * g2:32 * g2 + 32],
                    scalar1=spx[:, 4, 32 + g2:33 + g2], scalar2=0.0,
                    op0=ALU.subtract, op1=ALU.max),
                    reads=[res("spx0"), res("spx4")], writes=[res("spx2")])
            for g2 in range(16):
                P.emit("t", lambda t, g2=g2: t.transpose(
                    psum[6][0:32, 128 * (g2 % 4):128 * (g2 % 4) + 128],
                    spx[:, 2, 32 * g2:32 * g2 + 32], ident_f[:]),
                    reads=[res("spx2"), res("ident_f")], writes=[PS[6]])
                if g2 % 4 == 3:
                    P.emit("v", lambda v, g2=g2: v.tensor_copy(
                        ATb[0:32, g2 - 3:g2 + 1, :].rearrange(
                            "p a t -> p (a t)"),
                        psum[6][0:32, :]),
                        reads=[PS[6]], writes=[res("ATb")])
            for half in range(2):
                for chh in range(4):
                    ch = half * 4 + chh
                    for hp in range(2):
                        g2 = ch * 2 + hp
                        P.emit("t", lambda t, g2=g2, hp=hp, chh=chh: t.matmul(
                            psum[7][64 * hp:64 * hp + 64,
                                    chh * 128:chh * 128 + 128],
                            vsm_loc[:, 64 * hp:64 * hp + 64],
                            ATb[0:32, g2, :], start=True, stop=True),
                            reads=[res("ATb"), res("vsm")], writes=[PS[7]])
                P.emit("s", lambda s, half=half, p0=p0: s.activation(
                    out=xlnA[:, :, :].rearrange("p k t -> p (k t)")[
                        :, p0 + half * 512:p0 + half * 512 + 512],
                    in_=psum[7][:, :], func=AF.Identity,
                    bias=st32[:, 5, 1:2]),
                    reads=[PS[7], res("stsl5")], writes=[res("xlnA")])
        av_flat = xlnA[:, :, :].rearrange("p k t -> p (k t)")
        for kb in range(8):
            for jj in range(2):
                for hp in range(2):
                    j = 2 * kb + jj
                    P.emit("v", lambda v, kb=kb, jj=jj, hp=hp, j=j: v.tensor_copy(
                        biga[64 * jj:64 * jj + 64,
                             kb * 2048 + hp * 1024:kb * 2048 + hp * 1024 + 1024],
                        av_flat[64 * hp:64 * hp + 64, j::16]),
                        reads=[res("xlnA")], writes=[res("biga")])
        dbg("scr0", biga[:, 0:2048], res("biga"))

        def evo(m, n, b):
            P.emit("v", lambda v, m=m, n=n, b=b: v.scalar_tensor_tensor(
                out=xres[:, m, n * CH:(n + 1) * CH], in0=psum[b][:, :],
                scalar=smallv(3, m), in1=xres[:, m, n * CH:(n + 1) * CH],
                op0=ALU.add, op1=ALU.add),
                reads=[PS[b], res("smalls")], writes=[res("xres")])
        mm_stream(ein["Wo"], D, D,
                  lambda k, n: biga[:, k * TC + n * CH:k * TC + (n + 1) * CH],
                  NCH, CH, evo, rhs_res=[res("biga")])
        dbg("xres0", xres[:, 0, :], res("xres"))

    def phase_ffn(biga, xlnA):
        ln_fm(lambda k, n: xres[:, k, n * CH:(n + 1) * CH],
              lambda k, n: xlnA[:, k, n * CH:(n + 1) * CH], None,
              in_res=[res("xres")], out_res=[res("xlnA")])

        def h1v(m):
            return biga[:, m * CH:(m + 1) * CH]
        for n in range(NCH):
            def evh(m, nn, b):
                P.emit("s", lambda s, m=m, b=b: s.activation(
                    out=h1v(m)[:, :], in_=psum[b][:, :], func=AF.Gelu,
                    bias=b1sb[:, m:m + 1]),
                    reads=[PS[b], res("b1sb")], writes=[res("biga")])
            mm_stream(ein["w1"], D, NSTATE,
                      lambda k, nn, n=n: xlnA[:, k, n * CH:(n + 1) * CH],
                      1, CH, evh, rhs_res=[res("xlnA")])

            def evy(m, nn, b, n=n):
                P.emit("v", lambda v, m=m, b=b, n=n: v.scalar_tensor_tensor(
                    out=xres[:, m, n * CH:(n + 1) * CH], in0=psum[b][:, :],
                    scalar=smallv(4, m), in1=xres[:, m, n * CH:(n + 1) * CH],
                    op0=ALU.add, op1=ALU.add),
                    reads=[PS[b], res("smalls")], writes=[res("xres")])
            mm_stream(ein["w2"], NSTATE, D, lambda k, nn: h1v(k)[:, :],
                      1, CH, evy, rhs_res=[res("biga")])

    def collective(kind, op, in_ap, out_ap, rname):
        return P.emit("g", lambda g: g.collective_compute(
            kind, op, replica_groups=[list(range(NCORE))],
            ins=[in_ap], outs=[out_ap]),
            reads=[res(rname + "_in")], writes=[res(rname + "_out")],
            sem_key="cc_" + rname)

    def step_A(dr, l, M):
        pfx = f"m{dr}{l}_"
        for i, nm in enumerate(("nw", "nb")):
            P.emit("q", lambda q, nm=nm, i=i: q.dma_start(
                out=M["mnorm"][:, i, :], in_=ein[pfx + nm][:]),
                writes=[res("mnorm")], dma=True)
        for n in range(NCH):
            for k in range(NK_D):
                pp = (n * NK_D + k) % 2
                slt = 3 if pp == 0 else 5
                P.emit("q", lambda q, k=k, n=n, slt=slt: q.dma_start(
                    out=st32[:, slt, :],
                    in_=dram[f"h_res_{dr}"][128 * k:128 * k + 128,
                                            n * CH:(n + 1) * CH]),
                    reads=[res(f"h_res_{dr}")], writes=[res(f"stsl{slt}")],
                    dma=True)
                if l > 0:
                    P.emit("q", lambda q, k=k, n=n, pp=pp: q.dma_start(
                        out=M["ucst"][:, 0, pp, :],
                        in_=dram[f"rs_out_{dr}"][128 * k:128 * k + 128,
                                                 n * CH:(n + 1) * CH]),
                        reads=[res(f"rs_{dr}_out")], writes=[res(f"ucst{pp}")],
                        dma=True)
                    P.emit("v", lambda v, k=k, n=n, slt=slt, pp=pp:
                           v.tensor_tensor(
                               out=xres[:, k, n * CH:(n + 1) * CH],
                               in0=st32[:, slt, :], in1=M["ucst"][:, 0, pp, :],
                               op=ALU.add),
                           reads=[res(f"stsl{slt}"), res(f"ucst{pp}")],
                           writes=[res("xres")])
                else:
                    P.emit("v", lambda v, k=k, n=n, slt=slt: v.tensor_copy(
                        xres[:, k, n * CH:(n + 1) * CH], st32[:, slt, :]),
                        reads=[res(f"stsl{slt}")], writes=[res("xres")])
        if l > 0:
            P.emit("g", lambda q: q.dma_start(
                out=dram[f"h_res_{dr}"].rearrange("(k p) t -> p k t", p=128),
                in_=xres[:]),
                reads=[res("xres")], writes=[res(f"h_res_{dr}")], dma=True)

        def post(n):
            P.emit("g", lambda q, n=n: q.dma_start(
                out=dram[f"ag_in_{dr}"][:, n * CH:(n + 1) * CH].rearrange(
                    "(k p) t -> p k t", p=128),
                in_=M["rhsb"][:, :, :]),
                reads=[res("rhsb")], writes=[res(f"ag_{dr}_in")], dma=True)
        ln_fm(lambda k, n: xres[:, k, n * CH:(n + 1) * CH],
              lambda k, n: M["rhsb"][:, k, :],
              (lambda k: M["mnorm"][:, 0, k:k + 1],
               lambda k: M["mnorm"][:, 1, k:k + 1]),
              in_res=[res("xres"), res("mnorm")], out_res=[res("rhsb")],
              post_chunk=post)
        collective("AllGather", ALU.bypass, dram[f"ag_in_{dr}"][:],
                   dram[f"ag_out_{dr}"][:], f"ag_{dr}")

    def step_B(dr, l, M):
        pfx = f"m{dr}{l}_"
        rag = res(f"ag_{dr}_out")
        rev = (dr == "b")
        P.emit("q", lambda q: q.dma_start(
            out=M["mamw"][:, 0:8], in_=ein[pfx + "cw"][:]),
            writes=[res("mamw")], dma=True)
        for nm, c0 in (("cb", 8), ("dtb", 10), ("Dp", 12)):
            P.emit("q", lambda q, nm=nm, c0=c0: q.dma_start(
                out=M["mamw"][:, c0:c0 + 2], in_=ein[pfx + nm][:]),
                writes=[res("mamw")], dma=True)
        P.emit("q", lambda q: q.dma_start(
            out=M["mamA"][:].rearrange("p two s -> p (two s)"),
            in_=ein[pfx + "Alog"][:]),
            writes=[res("mamA")], dma=True)
        P.emit("s", lambda s: s.activation(
            out=M["mamA"][:].rearrange("p two s -> p (two s)"),
            in_=M["mamA"][:].rearrange("p two s -> p (two s)"), func=AF.Exp),
            reads=[res("mamA")], writes=[res("mamA")])
        P.emit("v", lambda v: v.tensor_scalar_mul(
            M["mamA"][:].rearrange("p two s -> p (two s)"),
            M["mamA"][:].rearrange("p two s -> p (two s)"), -1.0),
            reads=[res("mamA")], writes=[res("mamA")])
        for wkey, sl in (("inWuT", 0), ("inWzT", 1)):
            for k in range(NK_D):
                P.emit("q", lambda q, wkey=wkey, k=k, sl=sl: q.dma_start(
                    out=wld[:, sl, 0:256],
                    in_=ein[pfx + wkey][128 * k:128 * k + 128, :]),
                    writes=[res(f"wld{sl}")], dma=True)
                P.emit("v", lambda v, k=k, sl=sl: v.tensor_copy(
                    wslab[:, sl, 256 * k:256 * k + 256], wld[:, sl, 0:256]),
                    reads=[res(f"wld{sl}")], writes=[res(f"wslab{sl}")])
        P.emit("q", lambda q: q.dma_start(
            out=wld[:, 0, 0:192].rearrange("p (two c) -> p two c", two=2),
            in_=ein[pfx + "xpWT"].rearrange("(two p) c -> p two c", p=128)),
            writes=[res("wld0")], dma=True)
        P.emit("v", lambda v: v.tensor_copy(
            wslab[:, 0, 2048:2240], wld[:, 0, 0:192]),
            reads=[res("wld0")], writes=[res("wslab0")])

        order = list(range(LSEQ // CH))
        if rev:
            order = order[::-1]
        rhs = M["rhsb"]
        pend_xp = []
        for ni, n in enumerate(order):
            while len(pend_xp) > 1:
                pend_xp.pop(0)()
            slot, toff = divmod(n * CH, TC)
            P.emit("q", lambda q, slot=slot, toff=toff: q.dma_start(
                out=rhs[:, :, :],
                in_=dram[f"ag_out_{dr}"][slot, :, toff:toff + CH].rearrange(
                    "(k p) t -> p k t", p=128)),
                reads=[rag], writes=[res("rhsb")], dma=True)
            for m in range(2):
                for k in range(NK_D):
                    P.emit("t", lambda t, m=m, k=k: t.matmul(
                        psum[m][:, :],
                        wslab[:, 0, 256 * k + 128 * m:256 * k + 128 * m + 128],
                        rhs[:, k, :], start=(k == 0), stop=(k == NK_D - 1)),
                        reads=[res("wslab0"), res("rhsb")], writes=[PS[m]])
                for k in range(NK_D):
                    P.emit("t", lambda t, m=m, k=k: t.matmul(
                        psum[2 + m][:, :],
                        wslab[:, 1, 256 * k + 128 * m:256 * k + 128 * m + 128],
                        rhs[:, k, :], start=(k == 0), stop=(k == NK_D - 1)),
                        reads=[res("wslab1"), res("rhsb")], writes=[PS[2 + m]])
            for m in range(2):
                P.emit("v", lambda v, m=m: v.tensor_copy(
                    M["zst2"][:, m, 0:CH], psum[2 + m][:, :]),
                    reads=[PS[2 + m]], writes=[res("zst2")])
            P.emit("g", lambda q, n=n: q.dma_start(
                out=dram[f"h_z_{dr}"][:, n * CH:(n + 1) * CH].rearrange(
                    "(m p) t -> p m t", p=128),
                in_=M["zst2"][:, :, 0:CH]),
                reads=[res("zst2")], writes=[res(f"z_{dr}")], dma=True)
            if ni == 0:
                P.emit("v", lambda v: v.memset(M["uin"][:], 0.0),
                    writes=[res("uin")])
            for m in range(2):
                P.emit("v", lambda v, m=m: v.tensor_copy(
                    M["uin"][:, m, 4:4 + CH], psum[m][:, :]),
                    reads=[PS[m]], writes=[res("uin")])
            for m in range(2):
                tapb = M["dA"] if m == 0 else M["dA2"]
                rtap = res(f"dA{m}")
                for j in range(KCV):
                    off = (1 + j) if not rev else (7 - j)
                    P.emit("s", lambda s, m=m, off=off, j=j, tapb=tapb:
                           s.activation(
                               out=tapb[:, j // 2, (j % 2) * CH:(j % 2) * CH + CH],
                               in_=M["uin"][:, m, off:off + CH],
                               func=AF.Identity,
                               scale=M["mamw"][:, 4 * m + j:4 * m + j + 1]),
                           reads=[res("uin"), res("mamw")], writes=[rtap])
                P.emit("v", lambda v, m=m, tapb=tapb: v.tensor_tensor(
                    out=M["hf"][:, m, 0:CH], in0=tapb[:, 0, 0:CH],
                    in1=tapb[:, 0, CH:2 * CH], op=ALU.add),
                    reads=[rtap], writes=[res("hf")])
                P.emit("v", lambda v, m=m, tapb=tapb: v.tensor_tensor(
                    out=M["hf"][:, m, CH:2 * CH], in0=tapb[:, 1, 0:CH],
                    in1=tapb[:, 1, CH:2 * CH], op=ALU.add),
                    reads=[rtap], writes=[res("hf")])
                P.emit("v", lambda v, m=m: v.tensor_tensor(
                    out=M["cvac"][:, m, :], in0=M["hf"][:, m, 0:CH],
                    in1=M["hf"][:, m, CH:2 * CH], op=ALU.add),
                    reads=[res("hf")], writes=[res("cvac")])
                P.emit("s", lambda s, m=m: s.activation(
                    out=M["cvac"][:, m, :], in_=M["cvac"][:, m, :],
                    func=AF.Silu, bias=M["mamw"][:, 8 + m:9 + m]),
                    reads=[res("mamw")], writes=[res("cvac")])
                P.emit("s", lambda s, m=m, ni=ni: s.copy(
                    out=M["ucst"][:, m, ni % 2, :], in_=M["cvac"][:, m, :]),
                    reads=[res("cvac")], writes=[res(f"ucst{ni % 2}")])
            if not rev:
                P.emit("v", lambda v: v.tensor_copy(
                    M["uin"][:, :, 1:4], M["uin"][:, :, 1 + CH:4 + CH]),
                    reads=[res("uin")], writes=[res("uin")])
            else:
                P.emit("v", lambda v: v.tensor_copy(
                    M["uin"][:, :, 4 + CH:7 + CH], M["uin"][:, :, 4:7]),
                    reads=[res("uin")], writes=[res("uin")])
            P.emit("g", lambda q, n=n, ni=ni: q.dma_start(
                out=dram[f"h_uc_{dr}"][:, n * CH:(n + 1) * CH].rearrange(
                    "(m p) t -> p m t", p=128),
                in_=M["ucst"][:, :, ni % 2, :]),
                reads=[res(f"ucst{ni % 2}")], writes=[res(f"uc_{dr}")],
                dma=True)
            def xproj(n=n, ni=ni):
                for k in range(2):
                    P.emit("t", lambda t, k=k, ni=ni: t.matmul(
                        psum[4][0:96, :],
                        wslab[:, 0, 2048 + 96 * k:2048 + 96 * k + 96],
                        M["ucst"][:, k, ni % 2, :], start=(k == 0),
                        stop=(k == 1)),
                        reads=[res("wslab0"), res(f"ucst{ni % 2}")],
                        writes=[PS[4]])
                P.emit("v", lambda v: v.tensor_copy(
                    M["Bb"][0:96, 0:CH], psum[4][0:96, :]),
                    reads=[PS[4]], writes=[res("Bb0")])
                P.emit("g", lambda q, n=n: q.dma_start(
                    out=dram[f"ar_in_{dr}"][:, n * CH:(n + 1) * CH],
                    in_=M["Bb"][0:96, 0:CH]),
                    reads=[res("Bb0")], writes=[res(f"ar_{dr}_in")], dma=True)
            pend_xp.append(xproj)
        while pend_xp:
            pend_xp.pop(0)()
        collective("AllReduce", ALU.add, dram[f"ar_in_{dr}"][:],
                   dram[f"ar_out_{dr}"][:], f"ar_{dr}")

    def step_C(dr, l, M):
        pfx = f"m{dr}{l}_"
        rev = (dr == "b")
        P.emit("q", lambda q: q.dma_start(
            out=wld[0:64, 0, 0:256], in_=ein[pfx + "dtWT"][:]),
            writes=[res("wld0")], dma=True)
        P.emit("v", lambda v: v.tensor_copy(
            wslab[0:64, 0, 0:256], wld[0:64, 0, 0:256]),
            reads=[res("wld0")], writes=[res("wslab0")])
        for k in range(2):
            for mg in range(2):
                P.emit("q", lambda q, k=k, mg=mg: q.dma_start(
                    out=wld[:, 1, 0:512],
                    in_=ein[pfx + "outWT"][128 * k:128 * k + 128,
                                           512 * mg:512 * mg + 512]),
                    writes=[res("wld1")], dma=True)
                P.emit("v", lambda v, k=k, mg=mg: v.tensor_copy(
                    wslab[:, 1, 1024 * k + 512 * mg:1024 * k + 512 * mg + 512],
                    wld[:, 1, 0:512]),
                    reads=[res("wld1")], writes=[res("wslab1")])
        P.emit("v", lambda v: v.memset(M["carry"][:], 0.0),
               writes=[res("carry")])

        chunks = list(range(NSC))
        if rev:
            chunks = chunks[::-1]
        for ci in chunks:
            tg0 = ci * SCH
            P.emit("q", lambda q, tg0=tg0: q.dma_start(
                out=M["xdt"][:, :],
                in_=dram[f"ar_out_{dr}"][0:64, tg0:tg0 + SCH]),
                reads=[res(f"ar_{dr}_out")], writes=[res("xdt")], dma=True)
            P.emit("q", lambda q, tg0=tg0: q.dma_start(
                out=M["xbb"][:, :],
                in_=dram[f"ar_out_{dr}"][64:80, tg0:tg0 + SCH]),
                reads=[res(f"ar_{dr}_out")], writes=[res("xbb")], dma=True)
            P.emit("q", lambda q, tg0=tg0: q.dma_start(
                out=M["xcc"][:, :],
                in_=dram[f"ar_out_{dr}"][80:96, tg0:tg0 + SCH]),
                reads=[res(f"ar_{dr}_out")], writes=[res("xcc")], dma=True)
            if CPARTS < 2:
                continue
            for m in range(2):
                for sub in range(2):
                    P.emit("t", lambda t, m=m, sub=sub: t.matmul(
                        psum[4 + m][:, :], wslab[0:64, 0, 128 * m:128 * m + 128],
                        M["xdt"][:, sub * CH:(sub + 1) * CH],
                        start=True, stop=True),
                        reads=[res("wslab0"), res("xdt")], writes=[PS[4 + m]])
                    abs_tok = P.emit("s", lambda s, m=m, sub=sub: s.activation(
                        out=M["hf"][:, m, sub * CH:(sub + 1) * CH],
                        in_=psum[4 + m][:, :], func=AF.Abs,
                        bias=M["mamw"][:, 10 + m:11 + m]),
                        reads=[PS[4 + m], res("mamw")], writes=[res("hf")])
                    P.emit("s", lambda s, m=m, sub=sub: s.activation(
                        out=M["hf"][:, m, sub * CH:(sub + 1) * CH],
                        in_=M["hf"][:, m, sub * CH:(sub + 1) * CH],
                        func=AF.Exp, scale=-1.0),
                        reads=[res("hf")], writes=[res("hf")])
                    P.emit("s", lambda s, m=m, sub=sub: s.activation(
                        out=M["hf"][:, m, sub * CH:(sub + 1) * CH],
                        in_=M["hf"][:, m, sub * CH:(sub + 1) * CH],
                        func=AF.Ln, bias=onepp[:, 0:1]),
                        reads=[res("hf"), res("onepp")], writes=[res("hf")])
                    P.emit("v", lambda v, m=m, sub=sub: v.tensor_scalar(
                        out=M["dtu"][:, m, sub * CH:(sub + 1) * CH],
                        in0=psum[4 + m][:, :],
                        scalar1=M["mamw"][:, 10 + m:11 + m], scalar2=0.0,
                        op0=ALU.add, op1=ALU.max),
                        reads=[PS[4 + m], res("mamw")], writes=[res("dtu")],
                        deps=[abs_tok])
                    P.emit("v", lambda v, m=m, sub=sub: v.tensor_tensor(
                        out=M["dt"][:, m, sub * CH:(sub + 1) * CH],
                        in0=M["dtu"][:, m, sub * CH:(sub + 1) * CH],
                        in1=M["hf"][:, m, sub * CH:(sub + 1) * CH],
                        op=ALU.add),
                        reads=[res("dtu"), res("hf")], writes=[res("dt")])
            if CPARTS < 3:
                continue
            P.emit("q", lambda q, tg0=tg0: q.dma_start(
                out=M["uc"][:, :, :],
                in_=dram[f"h_uc_{dr}"][:, tg0:tg0 + SCH].rearrange(
                    "(m p) t -> p m t", p=128)),
                reads=[res(f"uc_{dr}")], writes=[res("uc")], dma=True)
            P.emit("v", lambda v: v.tensor_tensor(
                out=M["dtu"][:, :, :], in0=M["dt"][:, :, :],
                in1=M["uc"][:, :, :], op=ALU.mult),
                reads=[res("dt"), res("uc")], writes=[res("dtu")])
            if CPARTS < 4:
                continue

            def emit_bc(s_):
                P.emit("t", lambda t, s_=s_: t.matmul(
                    psum67[:, 0:CH], M["bsel"][:, 128 * s_:128 * s_ + 128],
                    M["xbb"][:, 0:CH], start=True, stop=True),
                    reads=[res("bsel"), res("xbb")], writes=[PS[6]])
                P.emit("t", lambda t, s_=s_: t.matmul(
                    psum67[:, CH:2 * CH], M["bsel"][:, 128 * s_:128 * s_ + 128],
                    M["xbb"][:, CH:2 * CH], start=True, stop=True),
                    reads=[res("bsel"), res("xbb")], writes=[PS[7]])
                P.emit("t", lambda t, s_=s_: t.matmul(
                    psum45[:, 0:CH], M["bsel"][:, 128 * s_:128 * s_ + 128],
                    M["xcc"][:, 0:CH], start=True, stop=True),
                    reads=[res("bsel"), res("xcc")], writes=[PS[4]])
                P.emit("t", lambda t, s_=s_: t.matmul(
                    psum45[:, CH:2 * CH], M["bsel"][:, 128 * s_:128 * s_ + 128],
                    M["xcc"][:, CH:2 * CH], start=True, stop=True),
                    reads=[res("bsel"), res("xcc")], writes=[PS[5]])

            emit_bc(0)
            for s_ in range(SST):
                par = s_ % 2
                dAb = M["dA"] if par == 0 else M["dA2"]
                wb_ = M["w"] if par == 0 else M["w2b"]
                Bbb = M["Bb"] if par == 0 else M["Bb2"]
                Cbb = M["Cb"] if par == 0 else M["Cb2"]
                rdA = res(f"dA{par}")
                rw = res(f"w{par}")
                rB = res(f"Bb{par}")
                rC = res(f"Cb{par}")
                P.emit("s", lambda s, Bbb=Bbb: s.copy(out=Bbb[:, :],
                                                      in_=psum67[:, :]),
                       reads=[PS[6], PS[7]], writes=[rB])
                P.emit("s", lambda s, Cbb=Cbb: s.copy(out=Cbb[:, :],
                                                      in_=psum45[:, :]),
                       reads=[PS[4], PS[5]], writes=[rC])
                if s_ + 1 < SST:
                    emit_bc(s_ + 1)
                for m in range(2):
                    P.emit("s", lambda s, m=m, s_=s_, dAb=dAb: s.activation(
                        out=dAb[:, m, :], in_=M["dt"][:, m, :], func=AF.Exp,
                        scale=M["mamA"][:, m, s_:s_ + 1]),
                        reads=[res("dt"), res("mamA")], writes=[rdA])
                if CPARTS < 5:
                    continue
                def dbu_fn(v, Bbb=Bbb):
                    a1, b1 = broadcast_tensor_aps(
                        M["dtu"][:, :, :],
                        Bbb[:].rearrange("p (one t) -> p one t", one=1))
                    return v.tensor_tensor(out=M["dBu"][:, :, :], in0=a1,
                                           in1=b1, op=ALU.mult)
                P.emit("v", dbu_fn, reads=[res("dtu"), rB], writes=[res("dBu")])
                for m in range(2):
                    cc = M["carry"][:, 2 * s_ + m:2 * s_ + m + 1]
                    rcc = res(f"carry{s_}_{m}")
                    if not rev:
                        P.emit("v", lambda v, m=m, cc=cc, dAb=dAb:
                               v.tensor_tensor_scan(
                                   out=M["h"][:, m, :], data0=dAb[:, m, :],
                                   data1=M["dBu"][:, m, :], initial=cc,
                                   op0=ALU.mult, op1=ALU.add),
                               reads=[rdA, res("dBu"), rcc],
                               writes=[res("h")])
                    else:
                        P.emit("v", lambda v, m=m, cc=cc, dAb=dAb:
                               v.tensor_tensor_scan(
                                   out=M["h"][:, m, ::-1], data0=dAb[:, m, ::-1],
                                   data1=M["dBu"][:, m, ::-1], initial=cc,
                                   op0=ALU.mult, op1=ALU.add),
                               reads=[rdA, res("dBu"), rcc],
                               writes=[res("h")])
                col = (SCH - 1) if not rev else 0
                P.emit("v", lambda v, s_=s_, col=col: v.tensor_copy(
                    M["carry"][:, 2 * s_:2 * s_ + 2],
                    M["h"][:, :, col:col + 1].rearrange("p m t -> p (m t)")),
                    reads=[res("h")],
                    writes=[res(f"carry{s_}_0"), res(f"carry{s_}_1")])
                def cmul_fn(v, wb_=wb_, Cbb=Cbb):
                    a1, b1 = broadcast_tensor_aps(
                        M["h"][:, :, :],
                        Cbb[:].rearrange("p (one t) -> p one t", one=1))
                    return v.tensor_tensor(out=wb_[:, :, :], in0=a1, in1=b1,
                                           op=ALU.mult)
                P.emit("v", cmul_fn, reads=[res("h"), rC], writes=[rw])
                for m in range(2):
                    for sub in range(2):
                        P.emit("t", lambda t, m=m, sub=sub, s_=s_, wb_=wb_:
                               t.matmul(
                                   psum[2 * m + sub][:, :], ident_b[:],
                                   wb_[:, m, sub * CH:(sub + 1) * CH],
                                   start=(s_ == 0), stop=(s_ == SST - 1)),
                               reads=[rw, res("ident_b")],
                               writes=[PS[2 * m + sub]])
            if CPARTS < 7:
                continue
            P.emit("q", lambda q, tg0=tg0: q.dma_start(
                out=M["zst2"][:, :, :],
                in_=dram[f"h_z_{dr}"][:, tg0:tg0 + SCH].rearrange(
                    "(m p) t -> p m t", p=128)),
                reads=[res(f"z_{dr}")], writes=[res("zst2")], dma=True)
            for m in range(2):
                P.emit("s", lambda s, m=m: s.activation(
                    out=M["hf"][:, m, :], in_=M["uc"][:, m, :],
                    func=AF.Identity, scale=M["mamw"][:, 12 + m:13 + m]),
                    reads=[res("uc"), res("mamw")], writes=[res("hf")])
            for m in range(2):
                for sub in range(2):
                    P.emit("v", lambda v, m=m, sub=sub: v.tensor_tensor(
                        out=M["hf"][:, m, sub * CH:(sub + 1) * CH],
                        in0=M["hf"][:, m, sub * CH:(sub + 1) * CH],
                        in1=psum[2 * m + sub][:, :], op=ALU.add),
                        reads=[PS[2 * m + sub]], writes=[res("hf")])
            P.emit("s", lambda s: s.activation(
                out=M["dt"][:, :, :], in_=M["zst2"][:, :, :], func=AF.Silu),
                reads=[res("zst2")], writes=[res("dt")])
            P.emit("v", lambda v: v.tensor_tensor(
                out=M["yg"][:, :, :], in0=M["hf"][:, :, :],
                in1=M["dt"][:, :, :], op=ALU.mult),
                reads=[res("hf"), res("dt")], writes=[res("dBu")])
            if CPARTS < 8:
                continue
            slot, toff = divmod(tg0, TC)
            for m in range(8):
                bank = m % 4
                wbuf = M["w"] if bank < 2 else M["w2b"]
                rwb = res(f"w{0 if bank < 2 else 1}")
                for sub in range(2):
                    for k in range(2):
                        P.emit("t", lambda t, m=m, k=k, sub=sub, bank=bank:
                               t.matmul(
                                   psum[bank][:, :],
                                   wslab[:, 1, 1024 * k + 128 * m:
                                         1024 * k + 128 * m + 128],
                                   M["yg"][:, k, sub * CH:(sub + 1) * CH],
                                   start=(k == 0), stop=(k == 1)),
                               reads=[res("wslab1"), res("dBu")],
                               writes=[PS[bank]])
                    P.emit("v", lambda v, sub=sub, bank=bank, wbuf=wbuf:
                           v.tensor_copy(
                               wbuf[:, bank % 2, sub * CH:(sub + 1) * CH],
                               psum[bank][:, :]),
                           reads=[PS[bank]], writes=[rwb])
                    P.emit("g", lambda q, m=m, sub=sub, slot=slot, toff=toff,
                           bank=bank, wbuf=wbuf: q.dma_start(
                               out=dram[f"rs_in_{dr}"][
                                   slot, 128 * m:128 * m + 128,
                                   toff + sub * CH:toff + (sub + 1) * CH],
                               in_=wbuf[:, bank % 2, sub * CH:(sub + 1) * CH]),
                           reads=[rwb],
                           writes=[res(f"rs_{dr}_in")], dma=True)
        if CPARTS >= 8:
            collective("ReduceScatter", ALU.add, dram[f"rs_in_{dr}"][:],
                       dram[f"rs_out_{dr}"][:], f"rs_{dr}")

    def phase_mamba(M):
        M["bself"] = M["uc"][0:16, :, :].rearrange("p m t -> p (m t)")
        P.emit("g", lambda g: g.memset(M["bself"][:], 1.0),
               writes=[res("uc")])
        P.emit("g", lambda g: g.affine_select(
            out=M["bself"][:], in_=M["bself"][:], compare_op=ALU.is_ge,
            fill=0.0, base=0, pattern=[[1, SST * 128]],
            channel_multiplier=-128),
            reads=[res("uc")], writes=[res("uc")])
        P.emit("g", lambda g: g.affine_select(
            out=M["bself"][:], in_=M["bself"][:], compare_op=ALU.is_ge,
            fill=0.0, base=127, pattern=[[-1, SST * 128]],
            channel_multiplier=128),
            reads=[res("uc")], writes=[res("uc")])
        P.emit("v", lambda v: v.tensor_copy(M["bsel"][:], M["bself"][:]),
               reads=[res("uc")], writes=[res("bsel")])
        P.emit("g", lambda q: q.dma_start(
            out=dram["h_xres"].rearrange("(k p) t -> p k t", p=128), in_=xres[:]),
            reads=[res("xres")], writes=[res("h_xres")], dma=True)
        ln_fm(lambda k, n: xres[:, k, n * CH:(n + 1) * CH],
              lambda k, n: xres[:, k, n * CH:(n + 1) * CH], None,
              in_res=[res("xres")], out_res=[res("xres")])
        for dr in DIRS:
            P.emit("g", lambda q, dr=dr: q.dma_start(
                out=dram[f"h_res_{dr}"].rearrange("(k p) t -> p k t", p=128),
                in_=xres[:]),
                reads=[res("xres")], writes=[res(f"h_res_{dr}")], dma=True)
        for l in range(NL_RUN):
            for stp, snm in ((step_A, "A"), (step_B, "B"), (step_C, "C")):
                if snm not in MSTEPS:
                    continue
                for dr in DIRS:
                    stp(dr, l, M)

    def phase_merge(biga, xlnC, rsst):
        for di, dr in enumerate(DIRS):
            for n in range(NCH):
                for k in range(NK_D):
                    pp = (n * NK_D + k) % 2
                    slt = 3 if pp == 0 else 5
                    P.emit("q", lambda q, k=k, n=n, dr=dr, slt=slt: q.dma_start(
                        out=st32[:, slt, :],
                        in_=dram[f"h_res_{dr}"][128 * k:128 * k + 128,
                                                n * CH:(n + 1) * CH]),
                        reads=[res(f"h_res_{dr}")], writes=[res(f"stsl{slt}")],
                        dma=True)
                    P.emit("q", lambda q, k=k, n=n, dr=dr, pp=pp: q.dma_start(
                        out=rsst[:, pp, :],
                        in_=dram[f"rs_out_{dr}"][128 * k:128 * k + 128,
                                                 n * CH:(n + 1) * CH]),
                        reads=[res(f"rs_{dr}_out")], writes=[res(f"rsst{pp}")],
                        dma=True)
                    P.emit("v", lambda v, k=k, n=n, slt=slt, pp=pp:
                           v.tensor_tensor(
                               out=xres[:, k, n * CH:(n + 1) * CH],
                               in0=st32[:, slt, :], in1=rsst[:, pp, :],
                               op=ALU.add),
                           reads=[res(f"stsl{slt}"), res(f"rsst{pp}")],
                           writes=[res("xres")])
            dstt = (lambda k, n: biga[:, k * TC + n * CH:k * TC + (n + 1) * CH]) \
                if di == 0 else (lambda k, n: xlnC[:, k, n * CH:(n + 1) * CH])
            ln_fm(lambda k, n: xres[:, k, n * CH:(n + 1) * CH], dstt,
                  (lambda k: smallv(6, k), lambda k: smallv(7, k)),
                  in_res=[res("xres")],
                  out_res=[res("biga") if di == 0 else res("xlnC")])
        for n in range(NCH):
            for k in range(NK_D):
                P.emit("q", lambda q, k=k, n=n: q.dma_start(
                    out=xres[:, k, n * CH:(n + 1) * CH],
                    in_=dram["h_xres"][128 * k:128 * k + 128,
                                       n * CH:(n + 1) * CH]),
                    reads=[res("h_xres")], writes=[res("xres")], dma=True)

        def evm(m, n, b):
            P.emit("v", lambda v, m=m, n=n, b=b: v.scalar_tensor_tensor(
                out=xres[:, m, n * CH:(n + 1) * CH], in0=psum[b][:, :],
                scalar=smallv(5, m), in1=xres[:, m, n * CH:(n + 1) * CH],
                op0=ALU.add, op1=ALU.add),
                reads=[PS[b], res("smalls")], writes=[res("xres")])

        def rhs_m(k, n):
            if k < NK_D:
                return biga[:, k * TC + n * CH:k * TC + (n + 1) * CH]
            return xlnC[:, k - NK_D, n * CH:(n + 1) * CH]
        mm_stream(ein["mWT"], 2 * D, D, rhs_m, NCH, CH, evm,
                  rhs_res=[res("biga"), res("xlnC")])

    if "attn" in phases or "ffn1" in phases:
        with ExitStack() as sc1:
            def SB1(name, shape, dtype=F32):
                return sc1.enter_context(nc.sbuf_tensor(name, list(shape), dtype))
            biga = SB1("biga1", [128, 16384], BF16)
            xlnA = SB1("xlnA", [128, NK_D, TC], BF16)
            spx = SB1("spx", [128, 5, 512])
            kiota = SB1("kiota", [128, 512])
            ktb = SB1("ktb", [128, NK_D, NPAT], BF16)
            kp_sb = SB1("kp_sb", [128, NK_D, NPAT], BF16)
            vsm_loc = SB1("vsm_loc", [NPAT, 128], BF16)
            ATb = SB1("ATb", [32, NH, 128], BF16)
            ktloc = SB1("ktloc", [128, NPAT], BF16)
            rhsb1 = SB1("rhsb1", [128, 2, NK_D, CH], BF16)
            P.emit("g", lambda g: g.iota(
                out=kiota[:].rearrange("p (h s) -> p h s", s=NPAT),
                pattern=[[0, NH], [1, NPAT]], base=1, channel_multiplier=0,
                allow_small_or_imprecise_dtypes=True),
                writes=[res("kiota")])
            if "attn" in phases:
                phase_attn(biga, xlnA, spx, kiota, ktb, kp_sb, vsm_loc, ATb,
                           ktloc, rhsb1)
            if "ffn1" in phases:
                phase_ffn(biga, xlnA)
        P.fence()

    if "mamba" in phases:
        with ExitStack() as sc2:
            def SB2(name, shape, dtype=F32):
                return sc2.enter_context(nc.sbuf_tensor(name, list(shape), dtype))
            M = {}
            M = {
                "dt": SB2("m_dt", [128, 2, SCH]),
                "dtu": SB2("m_dtu", [128, 2, SCH], BF16),
                "dA": SB2("m_dA", [128, 2, SCH], BF16),
                "dA2": SB2("m_dA2", [128, 2, SCH], BF16),
                "dBu": SB2("m_dBu", [128, 2, SCH], BF16),
                "h": SB2("m_h", [128, 2, SCH], BF16),
                "hf": SB2("m_hf", [128, 2, SCH]),
                "Bb": SB2("m_Bb", [128, SCH], BF16),
                "Bb2": SB2("m_Bb2", [128, SCH], BF16),
                "Cb": SB2("m_Cb", [128, SCH], BF16),
                "Cb2": SB2("m_Cb2", [128, SCH], BF16),
                "w": SB2("m_w", [128, 2, SCH], BF16),
                "w2b": SB2("m_w2b", [128, 2, SCH], BF16),
                "uc": SB2("m_uc", [128, 2, SCH], BF16),
                "zst2": SB2("m_zst2", [128, 2, SCH], BF16),
                "carry": SB2("m_carry", [128, 2 * SST]),
                "xdt": SB2("m_xdt", [64, SCH], BF16),
                "xbb": SB2("m_xbb", [16, SCH], BF16),
                "xcc": SB2("m_xcc", [16, SCH], BF16),
                "bsel": SB2("m_bsel", [16, SST * 128], BF16),
                "rhsb": SB2("m_rhsb", [128, NK_D, CH], BF16),
                "uin": SB2("m_uin", [128, 2, CH + 8]),
                "cvac": SB2("m_cvac", [128, 2, CH]),
                "ucst": SB2("m_ucst", [128, 2, 2, CH], BF16),
                "mamw": SB2("m_mamw", [128, 16]),
                "mamA": SB2("m_mamA", [128, 2, SST]),
                "mnorm": SB2("m_mnorm", [128, 2, NK_D]),
            }
            M["yg"] = M["dBu"]
            phase_mamba(M)
        P.fence()
        with ExitStack() as sc3:
            def SB3(name, shape, dtype=F32):
                return sc3.enter_context(nc.sbuf_tensor(name, list(shape), dtype))
            biga3 = SB3("biga3", [128, 16384], BF16)
            xlnC = SB3("xlnC", [128, NK_D, TC], BF16)
            rsst = SB3("rsst", [128, 2, CH], BF16)
            phase_merge(biga3, xlnC, rsst)
            if "ffn2" in phases:
                phase_ffn(biga3, xlnC)
        P.fence()
    elif "ffn2" in phases:
        with ExitStack() as sc3:
            biga3 = sc3.enter_context(nc.sbuf_tensor("biga3", [128, 16384], BF16))
            xlnC = sc3.enter_context(nc.sbuf_tensor("xlnC", [128, NK_D, TC], BF16))
            phase_ffn(biga3, xlnC)
        P.fence()

    st_tok = P.emit("q", lambda q: q.dma_start(
        out=out_d.rearrange("(k p) t -> p k t", p=128), in_=xres[:]),
        reads=[res("xres")], dma=True)
    P.emit("g", lambda g: g.engine_nop(), deps=[st_tok])

    P.build(nc)
    stack0.close()
    return nc


def _pp(v, kt):
    return np.asarray(v).reshape(kt, 128).T


def _pp2(v, kt, w):
    return np.asarray(v).reshape(kt, 128, w).transpose(1, 0, 2).reshape(128, kt * w)


def _slice_inputs(I, c):
    m = {}
    m["x_in"] = I["L0_attn_a_query"][0, c * TC:(c + 1) * TC, :].T
    m["Wk"] = I["L0_attn_a_Wk"]; m["Wo"] = I["L0_attn_a_Wo"]
    hs = slice(128 * c, 128 * (c + 1))
    m["WqS"] = I["L0_attn_a_Wq"][:, hs]
    m["WkS"] = I["L0_attn_a_Wk"][:, hs]
    m["WvS"] = I["L0_attn_a_Wv"][:, hs]
    m["bqS"] = I["L0_attn_a_bq"][hs][:, None]
    m["bkS"] = I["L0_attn_a_bk"][hs][:, None]
    m["bvS"] = I["L0_attn_a_bv"][hs][:, None]
    m["bk"] = _pp(I["L0_attn_a_bk"], 8); m["bo"] = _pp(I["L0_attn_a_bo"], 8)
    m["kpT"] = I["L0_attn_a_key_param"][0].T
    m["w1"] = I["L0_ffn_w1"]; m["b1"] = _pp(I["L0_ffn_b1"], 32)
    m["w2"] = I["L0_ffn_w2"]; m["b2"] = _pp(I["L0_ffn_b2"], 8)
    m["mWT"] = I["L0_attn_b_merge_W"].T
    m["mb"] = _pp(I["L0_attn_b_merge_b"], 8)
    m["nfw"] = _pp(I["L0_attn_b_normf_w"], 8)
    m["nfb"] = _pp(I["L0_attn_b_normf_b"], 8)
    sl = slice(c * DLOC, (c + 1) * DLOC)
    for dr, tag in (("f", "fw"), ("b", "bw")):
        for l in range(NL):
            p = f"m{dr}{l}_"
            g = f"L0_attn_b_{tag}_"
            m[p + "nw"] = _pp(I[g + "norm_w"][l], 8)
            m[p + "nb"] = _pp(I[g + "norm_b"][l], 8)
            inW = I[g + "in_W"][l]
            m[p + "inWuT"] = inW[sl, :].T
            m[p + "inWzT"] = inW[DIN + c * DLOC:DIN + (c + 1) * DLOC, :].T
            m[p + "cw"] = _pp2(I[g + "conv_w"][l][sl, :], 2, KCV)
            m[p + "cb"] = _pp(I[g + "conv_b"][l][sl], 2)
            m[p + "xpWT"] = I[g + "xproj_W"][l][:, sl].T
            m[p + "dtWT"] = I[g + "dt_W"][l][sl, :].T
            m[p + "dtb"] = _pp(I[g + "dt_b"][l][sl], 2)
            m[p + "Alog"] = _pp2(I[g + "Alog"][l][sl, :], 2, SST)
            m[p + "Dp"] = _pp(I[g + "D"][l][sl], 2)
            m[p + "outWT"] = I[g + "out_W"][l][:, sl].T
    return {k: np.ascontiguousarray(v, dtype=np.float32) for k, v in m.items()}


def kernel(**inputs):
    nc = build_program()
    in_maps = [_slice_inputs(inputs, c) for c in range(NCORE)]
    r = run_bass_kernel_spmd(nc, in_maps, core_ids=list(range(NCORE)))
    outs = [np.asarray(r.results[c]["out"]) for c in range(NCORE)]
    full = np.concatenate([o.T for o in outs], axis=0)
    return full[None].astype(np.float32)


# version 14 (final)
# speedup vs baseline: 1.0304x; 1.0304x over previous
import math
from contextlib import ExitStack
import numpy as np
import concourse.bass as bass
from concourse.bass import broadcast_tensor_aps
import concourse.mybir as mybir
from concourse.bass_utils import run_bass_kernel_spmd

F32 = mybir.dt.float32
BF16 = mybir.dt.bfloat16
AF = mybir.ActivationFunctionType
ALU = mybir.AluOpType

NCORE = 8
D = 1024
LSEQ = 16384
TC = LSEQ // NCORE
NH, HD, NPAT = 16, 64, 32
NSTATE = 4096
DIN = 2048
DLOC = DIN // NCORE
SST = 16
RNK = 64
KCV = 4
NL = 4
EPS = 1e-5
NK_D = D // 128
CH = 512
NCH = TC // CH
SCH = 1024
NSC = LSEQ // SCH
DIRS = ("f", "b")

PHASES = ("attn", "ffn1", "mamba", "ffn2")
DBG = False
NL_RUN = NL
MSTEPS = ("A", "B", "C")
CPARTS = 9


class Res:
    __slots__ = ("writer", "readers", "name")

    def __init__(self, name=""):
        self.writer = None
        self.readers = []
        self.name = name


class Prog:
    ENGS = ("q", "g", "v", "s", "t")

    def __init__(self):
        self.ops = {e: [] for e in self.ENGS}
        self.count = {e: 0 for e in self.ENGS}
        self.fence_deps = {}

    def fence(self):
        self.fence_deps = {e: v for e, v in self.count.items()
                           if v and not e.startswith("_")}

    NDQ = 16

    def emit(self, eng, fn, reads=(), writes=(), deps=(), dma=False,
             sem_key=None):
        dd = {}

        def add(tok):
            if tok is None:
                return
            e, v = tok
            dd[e] = max(dd.get(e, 0), v)

        for e, v in self.fence_deps.items():
            dd[e] = max(dd.get(e, 0), v)
        for tok in deps:
            add(tok)
        for r in reads:
            add(r.writer)
        for w in writes:
            add(w.writer)
            for tok in w.readers:
                add(tok)
        if dma:
            di = self.count.get("_dmai", 0)
            self.count["_dmai"] = di + 1
            key = f"dq{di % self.NDQ}"
            prev = self.count.get(key, 0)
            if prev:
                add((key, prev))
            inc = 16
        else:
            key = sem_key or eng
            inc = 1
        self.count[key] = self.count.get(key, 0) + inc
        tok = (key, self.count[key])
        self.ops[eng].append((tuple(sorted(dd.items())), fn, inc, key))
        for r in reads:
            r.readers.append(tok)
        for w in writes:
            w.writer = tok
            w.readers = []
        return tok

    def build(self, nc):
        keys = sorted(k for k in self.count if not k.startswith("_"))
        with ExitStack() as es:
            sems = {k: es.enter_context(nc.semaphore(f"sem_{k}"))
                    for k in keys}
            with nc.Block() as block:
                def runner(ename):
                    def _(h):
                        water = {k: 0 for k in keys}
                        for deps, fn, inc, key in self.ops[ename]:
                            for e, v in deps:
                                if water.get(e, 0) < v:
                                    h.wait_ge(sems[e], v)
                                    water[e] = v
                            fn(h).then_inc(sems[key], inc)
                    return _

                block.sync(runner("q"))
                block.gpsimd(runner("g"))
                block.vector(runner("v"))
                block.scalar(runner("s"))
                block.tensor(runner("t"))


def bitonic_stages(n=32):
    out = []
    K = 2
    while K <= n:
        j = K // 2
        while j >= 1:
            out.append((K, j))
            j //= 2
        K *= 2
    return out


def _verify_bitonic():
    rng = np.random.default_rng(0)
    x = rng.standard_normal((64, 32))
    a = x.copy()
    for K, j in bitonic_stages(32):
        for i in np.arange(32)[(np.arange(32) & j) == 0]:
            p = i | j
            desc = (i & K) == 0
            hi = np.maximum(a[:, i], a[:, p]).copy()
            lo = np.minimum(a[:, i], a[:, p]).copy()
            a[:, i], a[:, p] = (hi, lo) if desc else (lo, hi)
    assert np.allclose(a, -np.sort(-x, axis=-1))


_verify_bitonic()


def build_program(phases=None):
    if phases is None:
        phases = PHASES
    nc = bass.Bass("TRN2", target_bir_lowering=False)
    P = Prog()
    R = {}

    def res(name):
        if name not in R:
            R[name] = Res(name)
        return R[name]

    ein = {}

    def EI(name, shape):
        ein[name] = nc.dram_tensor(name, list(shape), F32,
                                   kind="ExternalInput").ap()

    EI("x_in", [D, TC])
    out_d = nc.dram_tensor("out", [D, TC], F32, kind="ExternalOutput").ap()
    for w in ("Wk", "Wo"):
        EI(w, [D, D])
    for w in ("WqS", "WkS", "WvS"):
        EI(w, [D, 128])
    for b in ("bqS", "bkS", "bvS"):
        EI(b, [128, 1])
    for b in ("bk", "bo"):
        EI(b, [128, NK_D])
    EI("kpT", [D, NPAT])
    EI("w1", [D, NSTATE]); EI("b1", [128, 32])
    EI("w2", [NSTATE, D]); EI("b2", [128, NK_D])
    EI("mWT", [2 * D, D]); EI("mb", [128, NK_D])
    EI("nfw", [128, NK_D]); EI("nfb", [128, NK_D])
    for dr in DIRS:
        for l in range(NL):
            p = f"m{dr}{l}_"
            EI(p + "nw", [128, NK_D]); EI(p + "nb", [128, NK_D])
            EI(p + "inWuT", [D, DLOC]); EI(p + "inWzT", [D, DLOC])
            EI(p + "cw", [128, 2 * KCV]); EI(p + "cb", [128, 2])
            EI(p + "xpWT", [DLOC, 96])
            EI(p + "dtWT", [RNK, DLOC]); EI(p + "dtb", [128, 2])
            EI(p + "Alog", [128, 2 * SST]); EI(p + "Dp", [128, 2])
            EI(p + "outWT", [DLOC, D])

    dram = {}

    def DI(name, shape, dtype=F32, shared=False):
        dram[name] = nc.dram_tensor(
            name, list(shape), dtype, kind="Internal",
            addr_space="Shared" if shared else "Local").ap()

    DI("h_xres", [D, TC])
    DI("ag_in_at", [D, TC], BF16)
    DI("ag_out_at", [NCORE, D, TC], BF16, shared=True)
    for dr in DIRS:
        DI(f"h_res_{dr}", [D, TC])
        DI(f"h_z_{dr}", [DLOC, LSEQ], BF16)
        DI(f"h_uc_{dr}", [DLOC, LSEQ], BF16)
        DI(f"ag_in_{dr}", [D, TC], BF16)
        DI(f"ag_out_{dr}", [NCORE, D, TC], BF16, shared=True)
        DI(f"ar_in_{dr}", [96, LSEQ], BF16)
        DI(f"ar_out_{dr}", [96, LSEQ], BF16)
        DI(f"rs_in_{dr}", [NCORE, D, TC], BF16)
        DI(f"rs_out_{dr}", [D, TC], BF16)

    stack0 = ExitStack()

    def SBp(name, shape, dtype=F32):
        return stack0.enter_context(nc.sbuf_tensor(name, list(shape), dtype))

    xres = SBp("xres", [128, NK_D, TC])
    wslab = SBp("wslab", [128, 2, 4096], BF16)
    wld = SBp("wld", [128, 4, 512])
    st32 = SBp("st32", [128, 6, CH])
    stt = SBp("stt", [1, 4, CH])
    ones_r = SBp("ones_r", [1, 128])
    ones_rb = SBp("ones_rb", [1, 128], BF16)
    ones_c = SBp("ones_c", [128, 1])
    ident_b = SBp("ident_b", [128, 128], BF16)
    ident_f = SBp("ident_f", [128, 128])
    smalls = SBp("smalls", [128, 96])
    b1sb = SBp("b1sb", [128, 32])
    epsc = SBp("epsc", [128, 1])
    onepp = SBp("onepp", [128, 1])

    psum = [stack0.enter_context(nc.psum_tensor(f"pb{i}", [128, 512], F32))
            for i in range(4)]
    psum45 = stack0.enter_context(nc.psum_tensor("pb45", [128, 1024], F32))
    psum67 = stack0.enter_context(nc.psum_tensor("pb67", [128, 1024], F32))
    psum.extend([psum45[:, 0:512], psum45[:, 512:1024],
                 psum67[:, 0:512], psum67[:, 512:1024]])
    PS = [res(f"pb{i}") for i in range(8)]

    P.emit("v", lambda v: v.memset(ones_r[:], 1.0), writes=[res("ones_r")])
    P.emit("v", lambda v: v.memset(ones_rb[:], 1.0), writes=[res("ones_rb")])
    P.emit("v", lambda v: v.memset(ones_c[:], 1.0), writes=[res("ones_c")])
    P.emit("v", lambda v: v.memset(epsc[:], EPS), writes=[res("epsc")])
    P.emit("v", lambda v: v.memset(onepp[:], 1.0), writes=[res("onepp")])
    P.emit("g", lambda g: g.memset(ident_f[:], 0.0), writes=[res("ident_f")])
    P.emit("g", lambda g: g.affine_select(
        out=ident_f[:], in_=ident_f[:], compare_op=ALU.not_equal, fill=1.0,
        base=0, pattern=[[-1, 128]], channel_multiplier=1),
        writes=[res("ident_f")])
    P.emit("v", lambda v: v.tensor_copy(ident_b[:], ident_f[:]),
           reads=[res("ident_f")], writes=[res("ident_b")])
    P.emit("q", lambda q: q.dma_start(
        out=xres[:], in_=ein["x_in"].rearrange("(k p) t -> p k t", p=128)),
        writes=[res("xres")], dma=True)
    for nm, col in (("bk", 1), ("bo", 3),
                    ("b2", 4), ("mb", 5), ("nfw", 6), ("nfb", 7)):
        P.emit("q", lambda q, nm=nm, col=col: q.dma_start(
            out=smalls[:, 8 * col:8 * col + 8], in_=ein[nm][:]),
            writes=[res("smalls")], dma=True)
    P.emit("q", lambda q: q.dma_start(
        out=b1sb[:], in_=ein["b1"][:]),
        writes=[res("b1sb")], dma=True)

    def smallv(col, k):
        return smalls[:, 8 * col + k:8 * col + k + 1]

    dbg_outs = {}

    def dbg(name, ap, rr):
        if not DBG:
            return
        sh = list(ap.shape)
        d = nc.dram_tensor("dbg_" + name, sh, ap.dtype,
                           kind="ExternalOutput").ap()
        dbg_outs[name] = sh
        P.emit("q", lambda q: q.dma_start(out=d[:], in_=ap),
               reads=[rr], dma=True)

    def mm_stream(W_ap, K_total, M_total, rhs_get, n_chunks, nw, evac,
                  rhs_res=()):
        KT = (K_total + 127) // 128
        MB = M_total // 128
        for m in range(MB):
            sl = m % 2
            wr = res(f"wslab{sl}")
            for k in range(KT):
                kh = min(128, K_total - 128 * k)
                ls = (m * KT + k) % 4
                lr = res(f"wld{ls}")
                P.emit("q", lambda q, m=m, k=k, kh=kh, ls=ls: q.dma_start(
                    out=wld[0:kh, ls, 0:128],
                    in_=W_ap[128 * k:128 * k + kh, 128 * m:128 * m + 128]),
                    writes=[lr], dma=True)
                P.emit("v", lambda v, k=k, kh=kh, sl=sl, ls=ls: v.tensor_copy(
                    wslab[0:kh, sl, 128 * k:128 * k + 128], wld[0:kh, ls, 0:128]),
                    reads=[lr], writes=[wr])
            for n in range(n_chunks):
                bank = n % 4
                for k in range(KT):
                    kh = min(128, K_total - 128 * k)
                    P.emit("t", lambda t, n=n, k=k, kh=kh, sl=sl, bank=bank: t.matmul(
                        psum[bank][:, 0:nw],
                        wslab[0:kh, sl, 128 * k:128 * k + 128],
                        rhs_get(k, n), start=(k == 0), stop=(k == KT - 1)),
                        reads=[wr] + list(rhs_res), writes=[PS[bank]])
                evac(m, n, bank)

    def ln_fm(src_get, dst_get, affine, in_res=(), out_res=(),
              nk=NK_D, n_chunks=NCH, post_chunk=None):
        ir = list(in_res)
        orr = list(out_res)
        for n in range(n_chunks):
            for k in range(nk):
                P.emit("t", lambda t, k=k, n=n: t.matmul(
                    psum[6][0:1, :], ones_c[:], src_get(k, n),
                    start=(k == 0), stop=(k == nk - 1)),
                    reads=[res("ones_c")] + ir, writes=[PS[6]])
            for k in range(nk):
                P.emit("s", lambda s, k=k, n=n: s.activation(
                    out=st32[:, 0, :], in_=src_get(k, n), func=AF.Square),
                    reads=ir, writes=[res("stsl0")])
                P.emit("t", lambda t, k=k: t.matmul(
                    psum[4][0:1, :], ones_c[:], st32[:, 0, :],
                    start=(k == 0), stop=(k == nk - 1)),
                    reads=[res("stsl0"), res("ones_c")], writes=[PS[4]])
            Dn = float(nk * 128)
            c0 = n * CH
            P.emit("v", lambda v, c0=c0: v.tensor_scalar_mul(
                stt[0:1, 0, :], psum[6][0:1, :], 1.0 / Dn),
                reads=[PS[6]], writes=[res("stt")])
            P.emit("v", lambda v, c0=c0: v.tensor_scalar_mul(
                stt[0:1, 1, :], psum[4][0:1, :], 1.0 / Dn),
                reads=[PS[4]], writes=[res("stt")])
            P.emit("v", lambda v, c0=c0: v.tensor_tensor(
                out=stt[0:1, 2, :], in0=stt[0:1, 0, :],
                in1=stt[0:1, 0, :], op=ALU.mult),
                reads=[res("stt")], writes=[res("stt")])
            P.emit("v", lambda v, c0=c0: v.tensor_tensor(
                out=stt[0:1, 1, :], in0=stt[0:1, 1, :],
                in1=stt[0:1, 2, :], op=ALU.subtract),
                reads=[res("stt")], writes=[res("stt")])
            P.emit("s", lambda s, c0=c0: s.activation(
                out=stt[0:1, 2, :], in_=stt[0:1, 1, :],
                func=AF.Sqrt, bias=epsc[0:1, 0:1]),
                reads=[res("stt"), res("epsc")], writes=[res("stt")])
            P.emit("v", lambda v, c0=c0: v.reciprocal(
                out=stt[0:1, 3, :], in_=stt[0:1, 2, :]),
                reads=[res("stt")], writes=[res("stt")])
            P.emit("t", lambda t, c0=c0: t.matmul(
                psum[7][:, :], ones_r[:], stt[0:1, 0, :],
                start=True, stop=True),
                reads=[res("stt"), res("ones_r")], writes=[PS[7]])
            P.emit("t", lambda t, c0=c0: t.matmul(
                psum[5][:, :], ones_r[:], stt[0:1, 3, :],
                start=True, stop=True),
                reads=[res("stt"), res("ones_r")], writes=[PS[5]])
            for k in range(nk):
                P.emit("v", lambda v, k=k, n=n: v.tensor_tensor(
                    out=st32[:, 1, :], in0=src_get(k, n),
                    in1=psum[7][:, :], op=ALU.subtract),
                    reads=[PS[7]] + ir, writes=[res("stsl1")])
                if affine is None:
                    P.emit("v", lambda v, k=k, n=n: v.tensor_tensor(
                        out=dst_get(k, n), in0=st32[:, 1, :],
                        in1=psum[5][:, :], op=ALU.mult),
                        reads=[res("stsl1"), PS[5]], writes=orr)
                else:
                    wg, bg = affine
                    P.emit("v", lambda v, k=k, n=n: v.tensor_tensor(
                        out=st32[:, 2, :], in0=st32[:, 1, :],
                        in1=psum[5][:, :], op=ALU.mult),
                        reads=[res("stsl1"), PS[5]], writes=[res("stsl2")])
                    P.emit("v", lambda v, k=k, n=n: v.tensor_scalar(
                        out=dst_get(k, n), in0=st32[:, 2, :],
                        scalar1=wg(k), scalar2=bg(k), op0=ALU.mult, op1=ALU.add),
                        reads=[res("stsl2")] + ir, writes=orr)
            if post_chunk is not None:
                post_chunk(n)

    def phase_attn(biga, xlnA, spx, kiota, ktb, kp_sb, vsm_loc, ATb,
                   ktloc, rhsb1):
        ln_fm(lambda k, n: xres[:, k, n * CH:(n + 1) * CH],
              lambda k, n: xlnA[:, k, n * CH:(n + 1) * CH], None,
              in_res=[res("xres")], out_res=[res("xlnA")])
        P.emit("g", lambda q: q.dma_start(
            out=dram["ag_in_at"].rearrange("(k p) t -> p k t", p=128),
            in_=xlnA[:, :, :]),
            reads=[res("xlnA")], writes=[res("ag_at_in")], dma=True)
        collective("AllGather", ALU.bypass, dram["ag_in_at"][:],
                   dram["ag_out_at"][:], "ag_at")
        for k in range(NK_D):
            P.emit("q", lambda q, k=k: q.dma_start(
                out=st32[:, 3, 0:NPAT], in_=ein["kpT"][128 * k:128 * k + 128, :]),
                writes=[res("stsl3")], dma=True)
            P.emit("v", lambda v, k=k: v.tensor_copy(
                kp_sb[:, k, :], st32[:, 3, 0:NPAT]),
                reads=[res("stsl3")], writes=[res("kp_sb")])
        def evk(m, n, b):
            P.emit("s", lambda s, m=m, b=b: s.activation(
                out=ktb[:, m, :], in_=psum[b][:, 0:NPAT],
                func=AF.Identity, bias=smallv(1, m)),
                reads=[PS[b], res("smalls")], writes=[res("ktb")])
        mm_stream(ein["Wk"], D, D, lambda k, n: kp_sb[:, k, :], 1, NPAT, evk,
                  rhs_res=[res("kp_sb")])
        for k in range(NK_D):
            P.emit("q", lambda q, k=k: q.dma_start(
                out=wld[:, 0, 0:128],
                in_=ein["WkS"][128 * k:128 * k + 128, :]),
                writes=[res("wld0")], dma=True)
            P.emit("v", lambda v, k=k: v.tensor_copy(
                wslab[:, 0, 128 * k:128 * k + 128], wld[:, 0, 0:128]),
                reads=[res("wld0")], writes=[res("wslab0")])
            P.emit("t", lambda t, k=k: t.matmul(
                psum[4][:, 0:NPAT], wslab[:, 0, 128 * k:128 * k + 128],
                kp_sb[:, k, :], start=(k == 0), stop=(k == NK_D - 1)),
                reads=[res("wslab0"), res("kp_sb")], writes=[PS[4]])
        P.emit("q", lambda q: q.dma_start(
            out=st32[:, 4, 0:1], in_=ein["bkS"][:]),
            writes=[res("stsl4")], dma=True)
        P.emit("s", lambda s: s.activation(
            out=ktloc[:, :], in_=psum[4][:, 0:NPAT], func=AF.Identity,
            bias=st32[:, 4, 0:1]),
            reads=[PS[4], res("stsl4")], writes=[res("ktloc")])
        for k in range(NK_D):
            P.emit("q", lambda q, k=k: q.dma_start(
                out=wld[:, 0, 0:128],
                in_=ein["WvS"][128 * k:128 * k + 128, :]),
                writes=[res("wld0")], dma=True)
            P.emit("v", lambda v, k=k: v.tensor_copy(
                wslab[:, 0, 128 * k:128 * k + 128], wld[:, 0, 0:128]),
                reads=[res("wld0")], writes=[res("wslab0")])
            P.emit("t", lambda t, k=k: t.matmul(
                psum[5][0:NPAT, 0:128], ktb[:, k, :],
                wslab[:, 0, 128 * k:128 * k + 128],
                start=(k == 0), stop=(k == NK_D - 1)),
                reads=[res("wslab0"), res("ktb")], writes=[PS[5]])
        P.emit("v", lambda v: v.tensor_copy(vsm_loc[:, :], psum[5][0:NPAT, 0:128]),
               reads=[PS[5]], writes=[res("vsm")])
        P.emit("q", lambda q: q.dma_start(
            out=st32[:, 5, 0:1], in_=ein["bqS"][:]),
            writes=[res("stsl5")], dma=True)
        P.emit("q", lambda q: q.dma_start(
            out=st32[:, 5, 1:2], in_=ein["bvS"][:]),
            writes=[res("stsl5")], dma=True)
        for k in range(NK_D):
            P.emit("q", lambda q, k=k: q.dma_start(
                out=wld[:, 0, 0:128],
                in_=ein["WqS"][128 * k:128 * k + 128, :]),
                writes=[res("wld0")], dma=True)
            P.emit("v", lambda v, k=k: v.tensor_copy(
                wslab[:, 1, 128 * k:128 * k + 128], wld[:, 0, 0:128]),
                reads=[res("wld0")], writes=[res("wslab1")])
        for n in range(LSEQ // CH):
            slot, toff = divmod(n * CH, TC)
            pp = n % 2
            P.emit("q", lambda q, slot=slot, toff=toff, pp=pp: q.dma_start(
                out=rhsb1[:, pp, :, :],
                in_=dram["ag_out_at"][slot, :, toff:toff + CH].rearrange(
                    "(k p) t -> p k t", p=128)),
                reads=[res("ag_at_out")], writes=[res(f"rhsb1{pp}")], dma=True)
            bank = n % 4
            for k in range(NK_D):
                P.emit("t", lambda t, k=k, bank=bank, pp=pp: t.matmul(
                    psum[bank][:, :], wslab[:, 1, 128 * k:128 * k + 128],
                    rhsb1[:, pp, k, :], start=(k == 0), stop=(k == NK_D - 1)),
                    reads=[res("wslab1"), res(f"rhsb1{pp}")], writes=[PS[bank]])
            P.emit("s", lambda s, n=n, bank=bank: s.activation(
                out=biga[:, n * CH:(n + 1) * CH], in_=psum[bank][:, :],
                func=AF.Identity, bias=st32[:, 5, 0:1]),
                reads=[PS[bank], res("stsl5")], writes=[res("biga")])
        scale = 1.0 / math.sqrt(HD)
        stages = bitonic_stages(NPAT)
        for og in range(LSEQ // 1024):
            p0 = og * 1024
            for ch in range(8):
                for hp in range(2):
                    P.emit("t", lambda t, ch=ch, hp=hp, p0=p0: t.matmul(
                        psum[5][:, (ch * 2 + hp) * 32:(ch * 2 + hp) * 32 + 32],
                        biga[64 * hp:64 * hp + 64,
                             p0 + ch * 128:p0 + ch * 128 + 128],
                        ktloc[64 * hp:64 * hp + 64, :],
                        start=True, stop=True),
                        reads=[res("biga"), res("ktloc")], writes=[PS[5]])
            P.emit("s", lambda s: s.activation(
                out=spx[:, 0, :], in_=psum[5][:, :], func=AF.Identity,
                scale=scale), reads=[PS[5]], writes=[res("spx0")])
            P.emit("v", lambda v: v.tensor_copy(spx[:, 1, :], spx[:, 0, :]),
                   reads=[res("spx0")], writes=[res("spx1")])
            cur, oth = 1, 2
            for (K, j) in stages:
                def rr(buf, K=K, j=j):
                    return spx[:, buf, :].rearrange(
                        "p (g two m b r) -> p g two m b r",
                        two=2, m=K // (2 * j), b=2, r=j)
                for d in (0, 1):
                    desc = (d == 0) or (K == NPAT)
                    mx = 0 if desc else 1
                    P.emit("v", lambda v, cur=cur, oth=oth, d=d, mx=mx, K=K, j=j:
                           v.tensor_tensor(
                               out=rr(oth, K, j)[:, :, d, :, mx, :],
                               in0=rr(cur, K, j)[:, :, d, :, 0, :],
                               in1=rr(cur, K, j)[:, :, d, :, 1, :], op=ALU.max),
                           reads=[res(f"spx{cur}")], writes=[res(f"spx{oth}")])
                    P.emit("v", lambda v, cur=cur, oth=oth, d=d, mx=mx, K=K, j=j:
                           v.tensor_tensor(
                               out=rr(oth, K, j)[:, :, d, :, 1 - mx, :],
                               in0=rr(cur, K, j)[:, :, d, :, 0, :],
                               in1=rr(cur, K, j)[:, :, d, :, 1, :], op=ALU.min),
                           reads=[res(f"spx{cur}")], writes=[res(f"spx{oth}")])
                cur, oth = oth, cur
            srt = cur
            seq = [(srt, 4), (4, 1), (1, 4), (4, 1), (1, 4)]
            for step, dshift in enumerate((1, 2, 4, 8, 16)):
                sbuf_, dbuf = seq[step]
                sv = spx[:, sbuf_, :].rearrange("p (h s) -> p h s", s=NPAT)
                dv = spx[:, dbuf, :].rearrange("p (h s) -> p h s", s=NPAT)
                P.emit("v", lambda v, sv=sv, dv=dv, ds=dshift: v.tensor_tensor(
                    out=dv[:, :, ds:], in0=sv[:, :, ds:],
                    in1=sv[:, :, 0:NPAT - ds], op=ALU.add),
                    reads=[res(f"spx{sbuf_}")], writes=[res(f"spx{dbuf}")])
                P.emit("v", lambda v, sv=sv, dv=dv, ds=dshift: v.tensor_copy(
                    dv[:, :, 0:ds], sv[:, :, 0:ds]),
                    reads=[res(f"spx{sbuf_}")], writes=[res(f"spx{dbuf}")])
            s3 = lambda sl: spx[:, sl, :].rearrange("p (h s) -> p h s", s=NPAT)
            P.emit("v", lambda v, srt=srt: v.tensor_tensor(
                out=spx[:, 1, :], in0=spx[:, srt, :], in1=kiota[:],
                op=ALU.mult),
                reads=[res(f"spx{srt}"), res("kiota")], writes=[res("spx1")])
            P.emit("v", lambda v: v.tensor_scalar_add(
                spx[:, 1, :], spx[:, 1, :], 1.0),
                reads=[res("spx1")], writes=[res("spx1")])
            P.emit("v", lambda v: v.tensor_tensor(
                out=spx[:, 1, :], in0=spx[:, 1, :], in1=spx[:, 4, :],
                op=ALU.is_gt),
                reads=[res("spx1"), res("spx4")], writes=[res("spx1")])
            P.emit("v", lambda v, s3=s3, srt=srt: v.tensor_tensor(
                out=s3(srt)[:, :, 0:NPAT - 1], in0=s3(1)[:, :, 0:NPAT - 1],
                in1=s3(1)[:, :, 1:NPAT], op=ALU.subtract),
                reads=[res("spx1")], writes=[res(f"spx{srt}")])
            P.emit("v", lambda v, s3=s3, srt=srt: v.tensor_copy(
                s3(srt)[:, :, NPAT - 1:NPAT], s3(1)[:, :, NPAT - 1:NPAT]),
                reads=[res("spx1")], writes=[res(f"spx{srt}")])
            P.emit("v", lambda v, srt=srt: v.tensor_tensor(
                out=spx[:, srt, :], in0=spx[:, srt, :], in1=spx[:, 4, :],
                op=ALU.mult),
                reads=[res(f"spx{srt}"), res("spx4")], writes=[res(f"spx{srt}")])
            P.emit("v", lambda v, s3=s3: v.tensor_reduce(
                out=spx[:, 4, 0:NH], in_=s3(1),
                axis=mybir.AxisListType.X, op=ALU.add),
                reads=[res("spx1")], writes=[res("spx4")])
            P.emit("v", lambda v, s3=s3, srt=srt: v.tensor_reduce(
                out=spx[:, 4, 16:16 + NH], in_=s3(srt),
                axis=mybir.AxisListType.X, op=ALU.add),
                reads=[res(f"spx{srt}")], writes=[res("spx4")])
            P.emit("v", lambda v: v.tensor_scalar_add(
                spx[:, 4, 16:16 + NH], spx[:, 4, 16:16 + NH], -1.0),
                reads=[res("spx4")], writes=[res("spx4")])
            P.emit("v", lambda v: v.reciprocal(
                out=spx[:, 4, 32:32 + NH], in_=spx[:, 4, 0:NH]),
                reads=[res("spx4")], writes=[res("spx4")])
            P.emit("v", lambda v: v.tensor_tensor(
                out=spx[:, 4, 32:32 + NH], in0=spx[:, 4, 16:16 + NH],
                in1=spx[:, 4, 32:32 + NH], op=ALU.mult),
                reads=[res("spx4")], writes=[res("spx4")])
            for g2 in range(16):
                P.emit("v", lambda v, g2=g2: v.tensor_scalar(
                    out=spx[:, 2, 32 * g2:32 * g2 + 32],
                    in0=spx[:, 0, 32 * g2:32 * g2 + 32],
                    scalar1=spx[:, 4, 32 + g2:33 + g2], scalar2=0.0,
                    op0=ALU.subtract, op1=ALU.max),
                    reads=[res("spx0"), res("spx4")], writes=[res("spx2")])
            for g2 in range(16):
                P.emit("t", lambda t, g2=g2: t.transpose(
                    psum[6][0:32, 128 * (g2 % 4):128 * (g2 % 4) + 128],
                    spx[:, 2, 32 * g2:32 * g2 + 32], ident_f[:]),
                    reads=[res("spx2"), res("ident_f")], writes=[PS[6]])
                if g2 % 4 == 3:
                    P.emit("v", lambda v, g2=g2: v.tensor_copy(
                        ATb[0:32, g2 - 3:g2 + 1, :].rearrange(
                            "p a t -> p (a t)"),
                        psum[6][0:32, :]),
                        reads=[PS[6]], writes=[res("ATb")])
            for half in range(2):
                for chh in range(4):
                    ch = half * 4 + chh
                    for hp in range(2):
                        g2 = ch * 2 + hp
                        P.emit("t", lambda t, g2=g2, hp=hp, chh=chh: t.matmul(
                            psum[7][64 * hp:64 * hp + 64,
                                    chh * 128:chh * 128 + 128],
                            vsm_loc[:, 64 * hp:64 * hp + 64],
                            ATb[0:32, g2, :], start=True, stop=True),
                            reads=[res("ATb"), res("vsm")], writes=[PS[7]])
                P.emit("s", lambda s, half=half, p0=p0: s.activation(
                    out=xlnA[:, :, :].rearrange("p k t -> p (k t)")[
                        :, p0 + half * 512:p0 + half * 512 + 512],
                    in_=psum[7][:, :], func=AF.Identity,
                    bias=st32[:, 5, 1:2]),
                    reads=[PS[7], res("stsl5")], writes=[res("xlnA")])
        av_flat = xlnA[:, :, :].rearrange("p k t -> p (k t)")
        for kb in range(8):
            for jj in range(2):
                for hp in range(2):
                    j = 2 * kb + jj
                    P.emit("v", lambda v, kb=kb, jj=jj, hp=hp, j=j: v.tensor_copy(
                        biga[64 * jj:64 * jj + 64,
                             kb * 2048 + hp * 1024:kb * 2048 + hp * 1024 + 1024],
                        av_flat[64 * hp:64 * hp + 64, j::16]),
                        reads=[res("xlnA")], writes=[res("biga")])
        dbg("scr0", biga[:, 0:2048], res("biga"))

        def evo(m, n, b):
            P.emit("v", lambda v, m=m, n=n, b=b: v.scalar_tensor_tensor(
                out=xres[:, m, n * CH:(n + 1) * CH], in0=psum[b][:, :],
                scalar=smallv(3, m), in1=xres[:, m, n * CH:(n + 1) * CH],
                op0=ALU.add, op1=ALU.add),
                reads=[PS[b], res("smalls")], writes=[res("xres")])
        mm_stream(ein["Wo"], D, D,
                  lambda k, n: biga[:, k * TC + n * CH:k * TC + (n + 1) * CH],
                  NCH, CH, evo, rhs_res=[res("biga")])
        dbg("xres0", xres[:, 0, :], res("xres"))

    def phase_ffn(biga, xlnA):
        ln_fm(lambda k, n: xres[:, k, n * CH:(n + 1) * CH],
              lambda k, n: xlnA[:, k, n * CH:(n + 1) * CH], None,
              in_res=[res("xres")], out_res=[res("xlnA")])

        def h1v(m):
            return biga[:, m * CH:(m + 1) * CH]
        for n in range(NCH):
            def evh(m, nn, b):
                P.emit("s", lambda s, m=m, b=b: s.activation(
                    out=h1v(m)[:, :], in_=psum[b][:, :], func=AF.Gelu,
                    bias=b1sb[:, m:m + 1]),
                    reads=[PS[b], res("b1sb")], writes=[res("biga")])
            mm_stream(ein["w1"], D, NSTATE,
                      lambda k, nn, n=n: xlnA[:, k, n * CH:(n + 1) * CH],
                      1, CH, evh, rhs_res=[res("xlnA")])

            def evy(m, nn, b, n=n):
                P.emit("v", lambda v, m=m, b=b, n=n: v.scalar_tensor_tensor(
                    out=xres[:, m, n * CH:(n + 1) * CH], in0=psum[b][:, :],
                    scalar=smallv(4, m), in1=xres[:, m, n * CH:(n + 1) * CH],
                    op0=ALU.add, op1=ALU.add),
                    reads=[PS[b], res("smalls")], writes=[res("xres")])
            mm_stream(ein["w2"], NSTATE, D, lambda k, nn: h1v(k)[:, :],
                      1, CH, evy, rhs_res=[res("biga")])

    def collective(kind, op, in_ap, out_ap, rname):
        return P.emit("g", lambda g: g.collective_compute(
            kind, op, replica_groups=[list(range(NCORE))],
            ins=[in_ap], outs=[out_ap]),
            reads=[res(rname + "_in")], writes=[res(rname + "_out")],
            sem_key="cc_" + rname)

    def step_A(dr, l, M):
        pfx = f"m{dr}{l}_"
        for i, nm in enumerate(("nw", "nb")):
            P.emit("q", lambda q, nm=nm, i=i: q.dma_start(
                out=M["mnorm"][:, i, :], in_=ein[pfx + nm][:]),
                writes=[res("mnorm")], dma=True)
        for n in range(NCH):
            for k in range(NK_D):
                pp = (n * NK_D + k) % 2
                slt = 3 if pp == 0 else 5
                P.emit("q", lambda q, k=k, n=n, slt=slt: q.dma_start(
                    out=st32[:, slt, :],
                    in_=dram[f"h_res_{dr}"][128 * k:128 * k + 128,
                                            n * CH:(n + 1) * CH]),
                    reads=[res(f"h_res_{dr}")], writes=[res(f"stsl{slt}")],
                    dma=True)
                if l > 0:
                    P.emit("q", lambda q, k=k, n=n, pp=pp: q.dma_start(
                        out=M["ucst"][:, 0, pp, :],
                        in_=dram[f"rs_out_{dr}"][128 * k:128 * k + 128,
                                                 n * CH:(n + 1) * CH]),
                        reads=[res(f"rs_{dr}_out")], writes=[res(f"ucst{pp}")],
                        dma=True)
                    P.emit("v", lambda v, k=k, n=n, slt=slt, pp=pp:
                           v.tensor_tensor(
                               out=xres[:, k, n * CH:(n + 1) * CH],
                               in0=st32[:, slt, :], in1=M["ucst"][:, 0, pp, :],
                               op=ALU.add),
                           reads=[res(f"stsl{slt}"), res(f"ucst{pp}")],
                           writes=[res("xres")])
                else:
                    P.emit("v", lambda v, k=k, n=n, slt=slt: v.tensor_copy(
                        xres[:, k, n * CH:(n + 1) * CH], st32[:, slt, :]),
                        reads=[res(f"stsl{slt}")], writes=[res("xres")])
        if l > 0:
            P.emit("g", lambda q: q.dma_start(
                out=dram[f"h_res_{dr}"].rearrange("(k p) t -> p k t", p=128),
                in_=xres[:]),
                reads=[res("xres")], writes=[res(f"h_res_{dr}")], dma=True)

        def post(n):
            P.emit("g", lambda q, n=n: q.dma_start(
                out=dram[f"ag_in_{dr}"][:, n * CH:(n + 1) * CH].rearrange(
                    "(k p) t -> p k t", p=128),
                in_=M["rhsb"][:, :, :]),
                reads=[res("rhsb")], writes=[res(f"ag_{dr}_in")], dma=True)
        ln_fm(lambda k, n: xres[:, k, n * CH:(n + 1) * CH],
              lambda k, n: M["rhsb"][:, k, :],
              (lambda k: M["mnorm"][:, 0, k:k + 1],
               lambda k: M["mnorm"][:, 1, k:k + 1]),
              in_res=[res("xres"), res("mnorm")], out_res=[res("rhsb")],
              post_chunk=post)
        collective("AllGather", ALU.bypass, dram[f"ag_in_{dr}"][:],
                   dram[f"ag_out_{dr}"][:], f"ag_{dr}")

    def step_B(dr, l, M):
        pfx = f"m{dr}{l}_"
        rag = res(f"ag_{dr}_out")
        rev = (dr == "b")
        P.emit("q", lambda q: q.dma_start(
            out=M["mamw"][:, 0:8], in_=ein[pfx + "cw"][:]),
            writes=[res("mamw")], dma=True)
        for nm, c0 in (("cb", 8), ("dtb", 10), ("Dp", 12)):
            P.emit("q", lambda q, nm=nm, c0=c0: q.dma_start(
                out=M["mamw"][:, c0:c0 + 2], in_=ein[pfx + nm][:]),
                writes=[res("mamw")], dma=True)
        P.emit("q", lambda q: q.dma_start(
            out=M["mamA"][:].rearrange("p two s -> p (two s)"),
            in_=ein[pfx + "Alog"][:]),
            writes=[res("mamA")], dma=True)
        P.emit("s", lambda s: s.activation(
            out=M["mamA"][:].rearrange("p two s -> p (two s)"),
            in_=M["mamA"][:].rearrange("p two s -> p (two s)"), func=AF.Exp),
            reads=[res("mamA")], writes=[res("mamA")])
        P.emit("v", lambda v: v.tensor_scalar_mul(
            M["mamA"][:].rearrange("p two s -> p (two s)"),
            M["mamA"][:].rearrange("p two s -> p (two s)"), -1.0),
            reads=[res("mamA")], writes=[res("mamA")])
        for wkey, sl in (("inWuT", 0), ("inWzT", 1)):
            for k in range(NK_D):
                P.emit("q", lambda q, wkey=wkey, k=k, sl=sl: q.dma_start(
                    out=wld[:, sl, 0:256],
                    in_=ein[pfx + wkey][128 * k:128 * k + 128, :]),
                    writes=[res(f"wld{sl}")], dma=True)
                P.emit("v", lambda v, k=k, sl=sl: v.tensor_copy(
                    wslab[:, sl, 256 * k:256 * k + 256], wld[:, sl, 0:256]),
                    reads=[res(f"wld{sl}")], writes=[res(f"wslab{sl}")])
        P.emit("q", lambda q: q.dma_start(
            out=wld[:, 0, 0:192].rearrange("p (two c) -> p two c", two=2),
            in_=ein[pfx + "xpWT"].rearrange("(two p) c -> p two c", p=128)),
            writes=[res("wld0")], dma=True)
        P.emit("v", lambda v: v.tensor_copy(
            wslab[:, 0, 2048:2240], wld[:, 0, 0:192]),
            reads=[res("wld0")], writes=[res("wslab0")])

        order = list(range(LSEQ // CH))
        if rev:
            order = order[::-1]
        rhs = M["rhsb"]
        pend_xp = []
        for ni, n in enumerate(order):
            while len(pend_xp) > 1:
                pend_xp.pop(0)()
            slot, toff = divmod(n * CH, TC)
            for hh in range(2):
                P.emit("q", lambda q, slot=slot, toff=toff, hh=hh: q.dma_start(
                    out=rhs[:, 4 * hh:4 * hh + 4, :],
                    in_=dram[f"ag_out_{dr}"][slot, 512 * hh:512 * hh + 512,
                                             toff:toff + CH].rearrange(
                        "(k p) t -> p k t", p=128)),
                    reads=[rag], writes=[res(f"rhsb{hh}")], dma=True)
            for hh in range(2):
                for m in range(2):
                    for k in range(4 * hh, 4 * hh + 4):
                        P.emit("t", lambda t, m=m, k=k, hh=hh: t.matmul(
                            psum[m][:, :],
                            wslab[:, 0, 256 * k + 128 * m:256 * k + 128 * m + 128],
                            rhs[:, k, :], start=(k == 0), stop=(k == NK_D - 1)),
                            reads=[res("wslab0"), res(f"rhsb{hh}")],
                            writes=[PS[m]])
                    for k in range(4 * hh, 4 * hh + 4):
                        P.emit("t", lambda t, m=m, k=k, hh=hh: t.matmul(
                            psum[2 + m][:, :],
                            wslab[:, 1, 256 * k + 128 * m:256 * k + 128 * m + 128],
                            rhs[:, k, :], start=(k == 0), stop=(k == NK_D - 1)),
                            reads=[res("wslab1"), res(f"rhsb{hh}")],
                            writes=[PS[2 + m]])
            for m in range(2):
                P.emit("v", lambda v, m=m: v.tensor_copy(
                    M["zst2"][:, m, 0:CH], psum[2 + m][:, :]),
                    reads=[PS[2 + m]], writes=[res("zst2")])
            P.emit("g", lambda q, n=n: q.dma_start(
                out=dram[f"h_z_{dr}"][:, n * CH:(n + 1) * CH].rearrange(
                    "(m p) t -> p m t", p=128),
                in_=M["zst2"][:, :, 0:CH]),
                reads=[res("zst2")], writes=[res(f"z_{dr}")], dma=True)
            if ni == 0:
                P.emit("v", lambda v: v.memset(M["uin"][:], 0.0),
                    writes=[res("uin")])
            for m in range(2):
                P.emit("v", lambda v, m=m: v.tensor_copy(
                    M["uin"][:, m, 4:4 + CH], psum[m][:, :]),
                    reads=[PS[m]], writes=[res("uin")])
            for m in range(2):
                tapb = M["dA"] if m == 0 else M["dA2"]
                rtap = res(f"dA{m}")
                for j in range(KCV):
                    off = (1 + j) if not rev else (7 - j)
                    P.emit("s", lambda s, m=m, off=off, j=j, tapb=tapb:
                           s.activation(
                               out=tapb[:, j // 2, (j % 2) * CH:(j % 2) * CH + CH],
                               in_=M["uin"][:, m, off:off + CH],
                               func=AF.Identity,
                               scale=M["mamw"][:, 4 * m + j:4 * m + j + 1]),
                           reads=[res("uin"), res("mamw")], writes=[rtap])
                P.emit("v", lambda v, m=m, tapb=tapb: v.tensor_tensor(
                    out=M["hf"][:, m, 0:CH], in0=tapb[:, 0, 0:CH],
                    in1=tapb[:, 0, CH:2 * CH], op=ALU.add),
                    reads=[rtap], writes=[res("hf")])
                P.emit("v", lambda v, m=m, tapb=tapb: v.tensor_tensor(
                    out=M["hf"][:, m, CH:2 * CH], in0=tapb[:, 1, 0:CH],
                    in1=tapb[:, 1, CH:2 * CH], op=ALU.add),
                    reads=[rtap], writes=[res("hf")])
                P.emit("v", lambda v, m=m: v.tensor_tensor(
                    out=M["hf"][:, m, 0:CH], in0=M["hf"][:, m, 0:CH],
                    in1=M["hf"][:, m, CH:2 * CH], op=ALU.add),
                    reads=[res("hf")], writes=[res("hf")])
                P.emit("s", lambda s, m=m, ni=ni: s.activation(
                    out=M["ucst"][:, m, ni % 2, :], in_=M["hf"][:, m, 0:CH],
                    func=AF.Silu, bias=M["mamw"][:, 8 + m:9 + m]),
                    reads=[res("hf"), res("mamw")],
                    writes=[res(f"ucst{ni % 2}")])
            if not rev:
                P.emit("v", lambda v: v.tensor_copy(
                    M["uin"][:, :, 1:4], M["uin"][:, :, 1 + CH:4 + CH]),
                    reads=[res("uin")], writes=[res("uin")])
            else:
                P.emit("v", lambda v: v.tensor_copy(
                    M["uin"][:, :, 4 + CH:7 + CH], M["uin"][:, :, 4:7]),
                    reads=[res("uin")], writes=[res("uin")])
            P.emit("g", lambda q, n=n, ni=ni: q.dma_start(
                out=dram[f"h_uc_{dr}"][:, n * CH:(n + 1) * CH].rearrange(
                    "(m p) t -> p m t", p=128),
                in_=M["ucst"][:, :, ni % 2, :]),
                reads=[res(f"ucst{ni % 2}")], writes=[res(f"uc_{dr}")],
                dma=True)
            def xproj(n=n, ni=ni):
                for k in range(2):
                    P.emit("t", lambda t, k=k, ni=ni: t.matmul(
                        psum[4][0:96, :],
                        wslab[:, 0, 2048 + 96 * k:2048 + 96 * k + 96],
                        M["ucst"][:, k, ni % 2, :], start=(k == 0),
                        stop=(k == 1)),
                        reads=[res("wslab0"), res(f"ucst{ni % 2}")],
                        writes=[PS[4]])
                P.emit("v", lambda v: v.tensor_copy(
                    M["Bb"][0:96, 0:CH], psum[4][0:96, :]),
                    reads=[PS[4]], writes=[res("Bb0")])
                P.emit("g", lambda q, n=n: q.dma_start(
                    out=dram[f"ar_in_{dr}"][:, n * CH:(n + 1) * CH],
                    in_=M["Bb"][0:96, 0:CH]),
                    reads=[res("Bb0")], writes=[res(f"ar_{dr}_in")], dma=True)
            pend_xp.append(xproj)
        while pend_xp:
            pend_xp.pop(0)()
        collective("AllReduce", ALU.add, dram[f"ar_in_{dr}"][:],
                   dram[f"ar_out_{dr}"][:], f"ar_{dr}")

    def step_C(dr, l, M):
        pfx = f"m{dr}{l}_"
        rev = (dr == "b")
        P.emit("q", lambda q: q.dma_start(
            out=wld[0:64, 0, 0:256], in_=ein[pfx + "dtWT"][:]),
            writes=[res("wld0")], dma=True)
        P.emit("v", lambda v: v.tensor_copy(
            wslab[0:64, 0, 0:256], wld[0:64, 0, 0:256]),
            reads=[res("wld0")], writes=[res("wslab0")])
        for k in range(2):
            for mg in range(2):
                P.emit("q", lambda q, k=k, mg=mg: q.dma_start(
                    out=wld[:, 1, 0:512],
                    in_=ein[pfx + "outWT"][128 * k:128 * k + 128,
                                           512 * mg:512 * mg + 512]),
                    writes=[res("wld1")], dma=True)
                P.emit("v", lambda v, k=k, mg=mg: v.tensor_copy(
                    wslab[:, 1, 1024 * k + 512 * mg:1024 * k + 512 * mg + 512],
                    wld[:, 1, 0:512]),
                    reads=[res("wld1")], writes=[res("wslab1")])
        P.emit("v", lambda v: v.memset(M["carry"][:], 0.0),
               writes=[res("carry")])

        chunks = list(range(NSC))
        if rev:
            chunks = chunks[::-1]

        def uc_load(ci_idx):
            tg = chunks[ci_idx] * SCH
            cp = ci_idx % 2
            P.emit("q", lambda q, tg=tg, cp=cp: q.dma_start(
                out=M["uc"][:, :, cp, :],
                in_=dram[f"h_uc_{dr}"][:, tg:tg + SCH].rearrange(
                    "(m p) t -> p m t", p=128)),
                reads=[res(f"uc_{dr}")], writes=[res(f"ucc{cp}")], dma=True)

        uc_load(0)
        for ci_idx, ci in enumerate(chunks):
            tg0 = ci * SCH
            ucp = ci_idx % 2
            ruc = res(f"ucc{ucp}")
            P.emit("q", lambda q, tg0=tg0: q.dma_start(
                out=M["xdt"][:, :],
                in_=dram[f"ar_out_{dr}"][0:64, tg0:tg0 + SCH]),
                reads=[res(f"ar_{dr}_out")], writes=[res("xdt")], dma=True)
            P.emit("q", lambda q, tg0=tg0: q.dma_start(
                out=M["xbb"][:, :],
                in_=dram[f"ar_out_{dr}"][64:80, tg0:tg0 + SCH]),
                reads=[res(f"ar_{dr}_out")], writes=[res("xbb")], dma=True)
            P.emit("q", lambda q, tg0=tg0: q.dma_start(
                out=M["xcc"][:, :],
                in_=dram[f"ar_out_{dr}"][80:96, tg0:tg0 + SCH]),
                reads=[res(f"ar_{dr}_out")], writes=[res("xcc")], dma=True)
            if CPARTS < 2:
                continue
            for m in range(2):
                for sub in range(2):
                    P.emit("t", lambda t, m=m, sub=sub: t.matmul(
                        psum[4 + m][:, :], wslab[0:64, 0, 128 * m:128 * m + 128],
                        M["xdt"][:, sub * CH:(sub + 1) * CH],
                        start=True, stop=True),
                        reads=[res("wslab0"), res("xdt")], writes=[PS[4 + m]])
                    abs_tok = P.emit("s", lambda s, m=m, sub=sub: s.activation(
                        out=M["hf"][:, m, sub * CH:(sub + 1) * CH],
                        in_=psum[4 + m][:, :], func=AF.Abs,
                        bias=M["mamw"][:, 10 + m:11 + m]),
                        reads=[PS[4 + m], res("mamw")], writes=[res("hf")])
                    P.emit("s", lambda s, m=m, sub=sub: s.activation(
                        out=M["hf"][:, m, sub * CH:(sub + 1) * CH],
                        in_=M["hf"][:, m, sub * CH:(sub + 1) * CH],
                        func=AF.Exp, scale=-1.0),
                        reads=[res("hf")], writes=[res("hf")])
                    P.emit("s", lambda s, m=m, sub=sub: s.activation(
                        out=M["hf"][:, m, sub * CH:(sub + 1) * CH],
                        in_=M["hf"][:, m, sub * CH:(sub + 1) * CH],
                        func=AF.Ln, bias=onepp[:, 0:1]),
                        reads=[res("hf"), res("onepp")], writes=[res("hf")])
                    P.emit("v", lambda v, m=m, sub=sub: v.tensor_scalar(
                        out=M["dtu"][:, m, sub * CH:(sub + 1) * CH],
                        in0=psum[4 + m][:, :],
                        scalar1=M["mamw"][:, 10 + m:11 + m], scalar2=0.0,
                        op0=ALU.add, op1=ALU.max),
                        reads=[PS[4 + m], res("mamw")], writes=[res("dtu")],
                        deps=[abs_tok])
                    P.emit("v", lambda v, m=m, sub=sub: v.tensor_tensor(
                        out=M["dt"][:, m, sub * CH:(sub + 1) * CH],
                        in0=M["dtu"][:, m, sub * CH:(sub + 1) * CH],
                        in1=M["hf"][:, m, sub * CH:(sub + 1) * CH],
                        op=ALU.add),
                        reads=[res("dtu"), res("hf")], writes=[res("dt")])
            if CPARTS < 3:
                continue
            if ci_idx + 1 < NSC:
                uc_load(ci_idx + 1)
            P.emit("v", lambda v, ucp=ucp: v.tensor_tensor(
                out=M["dtu"][:, :, :], in0=M["dt"][:, :, :],
                in1=M["uc"][:, :, ucp, :], op=ALU.mult),
                reads=[res("dt"), ruc], writes=[res("dtu")])
            if CPARTS < 4:
                continue

            def emit_bc(s_):
                P.emit("t", lambda t, s_=s_: t.matmul(
                    psum67[:, 0:CH], M["bsel"][:, 128 * s_:128 * s_ + 128],
                    M["xbb"][:, 0:CH], start=True, stop=True),
                    reads=[res("bsel"), res("xbb")], writes=[PS[6]])
                P.emit("t", lambda t, s_=s_: t.matmul(
                    psum67[:, CH:2 * CH], M["bsel"][:, 128 * s_:128 * s_ + 128],
                    M["xbb"][:, CH:2 * CH], start=True, stop=True),
                    reads=[res("bsel"), res("xbb")], writes=[PS[7]])
                P.emit("t", lambda t, s_=s_: t.matmul(
                    psum45[:, 0:CH], M["bsel"][:, 128 * s_:128 * s_ + 128],
                    M["xcc"][:, 0:CH], start=True, stop=True),
                    reads=[res("bsel"), res("xcc")], writes=[PS[4]])
                P.emit("t", lambda t, s_=s_: t.matmul(
                    psum45[:, CH:2 * CH], M["bsel"][:, 128 * s_:128 * s_ + 128],
                    M["xcc"][:, CH:2 * CH], start=True, stop=True),
                    reads=[res("bsel"), res("xcc")], writes=[PS[5]])

            emit_bc(0)
            for s_ in range(SST):
                par = s_ % 2
                dAb = M["dA"] if par == 0 else M["dA2"]
                wb_ = M["w"] if par == 0 else M["w2b"]
                Bbb = M["Bb"] if par == 0 else M["Bb2"]
                Cbb = M["Cb"] if par == 0 else M["Cb2"]
                rdA = res(f"dA{par}")
                rw = res(f"w{par}")
                rB = res(f"Bb{par}")
                rC = res(f"Cb{par}")
                P.emit("s", lambda s, Bbb=Bbb: s.copy(out=Bbb[:, :],
                                                      in_=psum67[:, :]),
                       reads=[PS[6], PS[7]], writes=[rB])
                P.emit("s", lambda s, Cbb=Cbb: s.copy(out=Cbb[:, :],
                                                      in_=psum45[:, :]),
                       reads=[PS[4], PS[5]], writes=[rC])
                if s_ + 1 < SST:
                    emit_bc(s_ + 1)
                for m in range(2):
                    P.emit("s", lambda s, m=m, s_=s_, dAb=dAb: s.activation(
                        out=dAb[:, m, :], in_=M["dt"][:, m, :], func=AF.Exp,
                        scale=M["mamA"][:, m, s_:s_ + 1]),
                        reads=[res("dt"), res("mamA")], writes=[rdA])
                if CPARTS < 5:
                    continue
                def dbu_fn(v, Bbb=Bbb):
                    a1, b1 = broadcast_tensor_aps(
                        M["dtu"][:, :, :],
                        Bbb[:].rearrange("p (one t) -> p one t", one=1))
                    return v.tensor_tensor(out=M["dBu"][:, :, :], in0=a1,
                                           in1=b1, op=ALU.mult)
                P.emit("v", dbu_fn, reads=[res("dtu"), rB], writes=[res("dBu")])
                for m in range(2):
                    cc = M["carry"][:, 2 * s_ + m:2 * s_ + m + 1]
                    rcc = res(f"carry{s_}_{m}")
                    if not rev:
                        P.emit("v", lambda v, m=m, cc=cc, dAb=dAb:
                               v.tensor_tensor_scan(
                                   out=M["h"][:, m, :], data0=dAb[:, m, :],
                                   data1=M["dBu"][:, m, :], initial=cc,
                                   op0=ALU.mult, op1=ALU.add),
                               reads=[rdA, res("dBu"), rcc],
                               writes=[res("h")])
                    else:
                        P.emit("v", lambda v, m=m, cc=cc, dAb=dAb:
                               v.tensor_tensor_scan(
                                   out=M["h"][:, m, ::-1], data0=dAb[:, m, ::-1],
                                   data1=M["dBu"][:, m, ::-1], initial=cc,
                                   op0=ALU.mult, op1=ALU.add),
                               reads=[rdA, res("dBu"), rcc],
                               writes=[res("h")])
                col = (SCH - 1) if not rev else 0
                P.emit("v", lambda v, s_=s_, col=col: v.tensor_copy(
                    M["carry"][:, 2 * s_:2 * s_ + 2],
                    M["h"][:, :, col:col + 1].rearrange("p m t -> p (m t)")),
                    reads=[res("h")],
                    writes=[res(f"carry{s_}_0"), res(f"carry{s_}_1")])
                def cmul_fn(v, wb_=wb_, Cbb=Cbb):
                    a1, b1 = broadcast_tensor_aps(
                        M["h"][:, :, :],
                        Cbb[:].rearrange("p (one t) -> p one t", one=1))
                    return v.tensor_tensor(out=wb_[:, :, :], in0=a1, in1=b1,
                                           op=ALU.mult)
                P.emit("v", cmul_fn, reads=[res("h"), rC], writes=[rw])
                for m in range(2):
                    for sub in range(2):
                        P.emit("t", lambda t, m=m, sub=sub, s_=s_, wb_=wb_:
                               t.matmul(
                                   psum[2 * m + sub][:, :], ident_b[:],
                                   wb_[:, m, sub * CH:(sub + 1) * CH],
                                   start=(s_ == 0), stop=(s_ == SST - 1)),
                               reads=[rw, res("ident_b")],
                               writes=[PS[2 * m + sub]])
            if CPARTS < 7:
                continue
            P.emit("q", lambda q, tg0=tg0: q.dma_start(
                out=M["zst2"][:, :, :],
                in_=dram[f"h_z_{dr}"][:, tg0:tg0 + SCH].rearrange(
                    "(m p) t -> p m t", p=128)),
                reads=[res(f"z_{dr}")], writes=[res("zst2")], dma=True)
            for m in range(2):
                P.emit("s", lambda s, m=m, ucp=ucp: s.activation(
                    out=M["hf"][:, m, :], in_=M["uc"][:, m, ucp, :],
                    func=AF.Identity, scale=M["mamw"][:, 12 + m:13 + m]),
                    reads=[ruc, res("mamw")], writes=[res("hf")])
            for m in range(2):
                for sub in range(2):
                    P.emit("v", lambda v, m=m, sub=sub: v.tensor_tensor(
                        out=M["hf"][:, m, sub * CH:(sub + 1) * CH],
                        in0=M["hf"][:, m, sub * CH:(sub + 1) * CH],
                        in1=psum[2 * m + sub][:, :], op=ALU.add),
                        reads=[PS[2 * m + sub]], writes=[res("hf")])
            P.emit("s", lambda s: s.activation(
                out=M["dt"][:, :, :], in_=M["zst2"][:, :, :], func=AF.Silu),
                reads=[res("zst2")], writes=[res("dt")])
            P.emit("v", lambda v: v.tensor_tensor(
                out=M["yg"][:, :, :], in0=M["hf"][:, :, :],
                in1=M["dt"][:, :, :], op=ALU.mult),
                reads=[res("hf"), res("dt")], writes=[res("dBu")])
            if CPARTS < 8:
                continue
            slot, toff = divmod(tg0, TC)
            for m in range(8):
                bank = m % 4
                wbuf = M["w"] if bank < 2 else M["w2b"]
                rwb = res(f"w{0 if bank < 2 else 1}")
                for sub in range(2):
                    for k in range(2):
                        P.emit("t", lambda t, m=m, k=k, sub=sub, bank=bank:
                               t.matmul(
                                   psum[bank][:, :],
                                   wslab[:, 1, 1024 * k + 128 * m:
                                         1024 * k + 128 * m + 128],
                                   M["yg"][:, k, sub * CH:(sub + 1) * CH],
                                   start=(k == 0), stop=(k == 1)),
                               reads=[res("wslab1"), res("dBu")],
                               writes=[PS[bank]])
                    P.emit("v", lambda v, sub=sub, bank=bank, wbuf=wbuf:
                           v.tensor_copy(
                               wbuf[:, bank % 2, sub * CH:(sub + 1) * CH],
                               psum[bank][:, :]),
                           reads=[PS[bank]], writes=[rwb])
                    P.emit("g", lambda q, m=m, sub=sub, slot=slot, toff=toff,
                           bank=bank, wbuf=wbuf: q.dma_start(
                               out=dram[f"rs_in_{dr}"][
                                   slot, 128 * m:128 * m + 128,
                                   toff + sub * CH:toff + (sub + 1) * CH],
                               in_=wbuf[:, bank % 2, sub * CH:(sub + 1) * CH]),
                           reads=[rwb],
                           writes=[res(f"rs_{dr}_in")], dma=True)
        if CPARTS >= 8:
            collective("ReduceScatter", ALU.add, dram[f"rs_in_{dr}"][:],
                       dram[f"rs_out_{dr}"][:], f"rs_{dr}")

    def phase_mamba(M):
        M["bself"] = M["uc"][0:16, :, :, :].rearrange("p m c t -> p (m c t)")[:, 0:SST * 128]
        P.emit("g", lambda g: g.memset(M["bself"][:], 1.0),
               writes=[res("ucc0"), res("ucc1")])
        P.emit("g", lambda g: g.affine_select(
            out=M["bself"][:], in_=M["bself"][:], compare_op=ALU.is_ge,
            fill=0.0, base=0, pattern=[[1, SST * 128]],
            channel_multiplier=-128),
            reads=[res("ucc0"), res("ucc1")],
            writes=[res("ucc0"), res("ucc1")])
        P.emit("g", lambda g: g.affine_select(
            out=M["bself"][:], in_=M["bself"][:], compare_op=ALU.is_ge,
            fill=0.0, base=127, pattern=[[-1, SST * 128]],
            channel_multiplier=128),
            reads=[res("ucc0"), res("ucc1")],
            writes=[res("ucc0"), res("ucc1")])
        P.emit("v", lambda v: v.tensor_copy(M["bsel"][:], M["bself"][:]),
               reads=[res("ucc0"), res("ucc1")], writes=[res("bsel")])
        P.emit("g", lambda q: q.dma_start(
            out=dram["h_xres"].rearrange("(k p) t -> p k t", p=128), in_=xres[:]),
            reads=[res("xres")], writes=[res("h_xres")], dma=True)
        ln_fm(lambda k, n: xres[:, k, n * CH:(n + 1) * CH],
              lambda k, n: xres[:, k, n * CH:(n + 1) * CH], None,
              in_res=[res("xres")], out_res=[res("xres")])
        for dr in DIRS:
            P.emit("g", lambda q, dr=dr: q.dma_start(
                out=dram[f"h_res_{dr}"].rearrange("(k p) t -> p k t", p=128),
                in_=xres[:]),
                reads=[res("xres")], writes=[res(f"h_res_{dr}")], dma=True)
        for l in range(NL_RUN):
            for stp, snm in ((step_A, "A"), (step_B, "B"), (step_C, "C")):
                if snm not in MSTEPS:
                    continue
                for dr in DIRS:
                    stp(dr, l, M)

    def phase_merge(biga, xlnC, rsst):
        for di, dr in enumerate(DIRS):
            for n in range(NCH):
                for k in range(NK_D):
                    pp = (n * NK_D + k) % 2
                    slt = 3 if pp == 0 else 5
                    P.emit("q", lambda q, k=k, n=n, dr=dr, slt=slt: q.dma_start(
                        out=st32[:, slt, :],
                        in_=dram[f"h_res_{dr}"][128 * k:128 * k + 128,
                                                n * CH:(n + 1) * CH]),
                        reads=[res(f"h_res_{dr}")], writes=[res(f"stsl{slt}")],
                        dma=True)
                    P.emit("q", lambda q, k=k, n=n, dr=dr, pp=pp: q.dma_start(
                        out=rsst[:, pp, :],
                        in_=dram[f"rs_out_{dr}"][128 * k:128 * k + 128,
                                                 n * CH:(n + 1) * CH]),
                        reads=[res(f"rs_{dr}_out")], writes=[res(f"rsst{pp}")],
                        dma=True)
                    P.emit("v", lambda v, k=k, n=n, slt=slt, pp=pp:
                           v.tensor_tensor(
                               out=xres[:, k, n * CH:(n + 1) * CH],
                               in0=st32[:, slt, :], in1=rsst[:, pp, :],
                               op=ALU.add),
                           reads=[res(f"stsl{slt}"), res(f"rsst{pp}")],
                           writes=[res("xres")])
            dstt = (lambda k, n: biga[:, k * TC + n * CH:k * TC + (n + 1) * CH]) \
                if di == 0 else (lambda k, n: xlnC[:, k, n * CH:(n + 1) * CH])
            ln_fm(lambda k, n: xres[:, k, n * CH:(n + 1) * CH], dstt,
                  (lambda k: smallv(6, k), lambda k: smallv(7, k)),
                  in_res=[res("xres")],
                  out_res=[res("biga") if di == 0 else res("xlnC")])
        for n in range(NCH):
            for k in range(NK_D):
                P.emit("q", lambda q, k=k, n=n: q.dma_start(
                    out=xres[:, k, n * CH:(n + 1) * CH],
                    in_=dram["h_xres"][128 * k:128 * k + 128,
                                       n * CH:(n + 1) * CH]),
                    reads=[res("h_xres")], writes=[res("xres")], dma=True)

        def evm(m, n, b):
            P.emit("v", lambda v, m=m, n=n, b=b: v.scalar_tensor_tensor(
                out=xres[:, m, n * CH:(n + 1) * CH], in0=psum[b][:, :],
                scalar=smallv(5, m), in1=xres[:, m, n * CH:(n + 1) * CH],
                op0=ALU.add, op1=ALU.add),
                reads=[PS[b], res("smalls")], writes=[res("xres")])

        def rhs_m(k, n):
            if k < NK_D:
                return biga[:, k * TC + n * CH:k * TC + (n + 1) * CH]
            return xlnC[:, k - NK_D, n * CH:(n + 1) * CH]
        mm_stream(ein["mWT"], 2 * D, D, rhs_m, NCH, CH, evm,
                  rhs_res=[res("biga"), res("xlnC")])

    if "attn" in phases or "ffn1" in phases:
        with ExitStack() as sc1:
            def SB1(name, shape, dtype=F32):
                return sc1.enter_context(nc.sbuf_tensor(name, list(shape), dtype))
            biga = SB1("biga1", [128, 16384], BF16)
            xlnA = SB1("xlnA", [128, NK_D, TC], BF16)
            spx = SB1("spx", [128, 5, 512])
            kiota = SB1("kiota", [128, 512])
            ktb = SB1("ktb", [128, NK_D, NPAT], BF16)
            kp_sb = SB1("kp_sb", [128, NK_D, NPAT], BF16)
            vsm_loc = SB1("vsm_loc", [NPAT, 128], BF16)
            ATb = SB1("ATb", [32, NH, 128], BF16)
            ktloc = SB1("ktloc", [128, NPAT], BF16)
            rhsb1 = SB1("rhsb1", [128, 2, NK_D, CH], BF16)
            P.emit("g", lambda g: g.iota(
                out=kiota[:].rearrange("p (h s) -> p h s", s=NPAT),
                pattern=[[0, NH], [1, NPAT]], base=1, channel_multiplier=0,
                allow_small_or_imprecise_dtypes=True),
                writes=[res("kiota")])
            if "attn" in phases:
                phase_attn(biga, xlnA, spx, kiota, ktb, kp_sb, vsm_loc, ATb,
                           ktloc, rhsb1)
            if "ffn1" in phases:
                phase_ffn(biga, xlnA)
        P.fence()

    if "mamba" in phases:
        with ExitStack() as sc2:
            def SB2(name, shape, dtype=F32):
                return sc2.enter_context(nc.sbuf_tensor(name, list(shape), dtype))
            M = {}
            M = {
                "dt": SB2("m_dt", [128, 2, SCH]),
                "dtu": SB2("m_dtu", [128, 2, SCH], BF16),
                "dA": SB2("m_dA", [128, 2, SCH], BF16),
                "dA2": SB2("m_dA2", [128, 2, SCH], BF16),
                "dBu": SB2("m_dBu", [128, 2, SCH], BF16),
                "h": SB2("m_h", [128, 2, SCH], BF16),
                "hf": SB2("m_hf", [128, 2, SCH]),
                "Bb": SB2("m_Bb", [128, SCH], BF16),
                "Bb2": SB2("m_Bb2", [128, SCH], BF16),
                "Cb": SB2("m_Cb", [128, SCH], BF16),
                "Cb2": SB2("m_Cb2", [128, SCH], BF16),
                "w": SB2("m_w", [128, 2, SCH], BF16),
                "w2b": SB2("m_w2b", [128, 2, SCH], BF16),
                "uc": SB2("m_uc", [128, 2, 2, SCH], BF16),
                "zst2": SB2("m_zst2", [128, 2, SCH], BF16),
                "carry": SB2("m_carry", [128, 2 * SST]),
                "xdt": SB2("m_xdt", [64, SCH], BF16),
                "xbb": SB2("m_xbb", [16, SCH], BF16),
                "xcc": SB2("m_xcc", [16, SCH], BF16),
                "bsel": SB2("m_bsel", [16, SST * 128], BF16),
                "rhsb": SB2("m_rhsb", [128, NK_D, CH], BF16),
                "uin": SB2("m_uin", [128, 2, CH + 8]),
                "ucst": SB2("m_ucst", [128, 2, 2, CH], BF16),
                "mamw": SB2("m_mamw", [128, 16]),
                "mamA": SB2("m_mamA", [128, 2, SST]),
                "mnorm": SB2("m_mnorm", [128, 2, NK_D]),
            }
            M["yg"] = M["dBu"]
            phase_mamba(M)
        P.fence()
        with ExitStack() as sc3:
            def SB3(name, shape, dtype=F32):
                return sc3.enter_context(nc.sbuf_tensor(name, list(shape), dtype))
            biga3 = SB3("biga3", [128, 16384], BF16)
            xlnC = SB3("xlnC", [128, NK_D, TC], BF16)
            rsst = SB3("rsst", [128, 2, CH], BF16)
            phase_merge(biga3, xlnC, rsst)
            if "ffn2" in phases:
                phase_ffn(biga3, xlnC)
        P.fence()
    elif "ffn2" in phases:
        with ExitStack() as sc3:
            biga3 = sc3.enter_context(nc.sbuf_tensor("biga3", [128, 16384], BF16))
            xlnC = sc3.enter_context(nc.sbuf_tensor("xlnC", [128, NK_D, TC], BF16))
            phase_ffn(biga3, xlnC)
        P.fence()

    st_tok = P.emit("q", lambda q: q.dma_start(
        out=out_d.rearrange("(k p) t -> p k t", p=128), in_=xres[:]),
        reads=[res("xres")], dma=True)
    P.emit("g", lambda g: g.engine_nop(), deps=[st_tok])

    P.build(nc)
    stack0.close()
    return nc


def _pp(v, kt):
    return np.asarray(v).reshape(kt, 128).T


def _pp2(v, kt, w):
    return np.asarray(v).reshape(kt, 128, w).transpose(1, 0, 2).reshape(128, kt * w)


def _slice_inputs(I, c):
    m = {}
    m["x_in"] = I["L0_attn_a_query"][0, c * TC:(c + 1) * TC, :].T
    m["Wk"] = I["L0_attn_a_Wk"]; m["Wo"] = I["L0_attn_a_Wo"]
    hs = slice(128 * c, 128 * (c + 1))
    m["WqS"] = I["L0_attn_a_Wq"][:, hs]
    m["WkS"] = I["L0_attn_a_Wk"][:, hs]
    m["WvS"] = I["L0_attn_a_Wv"][:, hs]
    m["bqS"] = I["L0_attn_a_bq"][hs][:, None]
    m["bkS"] = I["L0_attn_a_bk"][hs][:, None]
    m["bvS"] = I["L0_attn_a_bv"][hs][:, None]
    m["bk"] = _pp(I["L0_attn_a_bk"], 8); m["bo"] = _pp(I["L0_attn_a_bo"], 8)
    m["kpT"] = I["L0_attn_a_key_param"][0].T
    m["w1"] = I["L0_ffn_w1"]; m["b1"] = _pp(I["L0_ffn_b1"], 32)
    m["w2"] = I["L0_ffn_w2"]; m["b2"] = _pp(I["L0_ffn_b2"], 8)
    m["mWT"] = I["L0_attn_b_merge_W"].T
    m["mb"] = _pp(I["L0_attn_b_merge_b"], 8)
    m["nfw"] = _pp(I["L0_attn_b_normf_w"], 8)
    m["nfb"] = _pp(I["L0_attn_b_normf_b"], 8)
    sl = slice(c * DLOC, (c + 1) * DLOC)
    for dr, tag in (("f", "fw"), ("b", "bw")):
        for l in range(NL):
            p = f"m{dr}{l}_"
            g = f"L0_attn_b_{tag}_"
            m[p + "nw"] = _pp(I[g + "norm_w"][l], 8)
            m[p + "nb"] = _pp(I[g + "norm_b"][l], 8)
            inW = I[g + "in_W"][l]
            m[p + "inWuT"] = inW[sl, :].T
            m[p + "inWzT"] = inW[DIN + c * DLOC:DIN + (c + 1) * DLOC, :].T
            m[p + "cw"] = _pp2(I[g + "conv_w"][l][sl, :], 2, KCV)
            m[p + "cb"] = _pp(I[g + "conv_b"][l][sl], 2)
            m[p + "xpWT"] = I[g + "xproj_W"][l][:, sl].T
            m[p + "dtWT"] = I[g + "dt_W"][l][sl, :].T
            m[p + "dtb"] = _pp(I[g + "dt_b"][l][sl], 2)
            m[p + "Alog"] = _pp2(I[g + "Alog"][l][sl, :], 2, SST)
            m[p + "Dp"] = _pp(I[g + "D"][l][sl], 2)
            m[p + "outWT"] = I[g + "out_W"][l][:, sl].T
    return {k: np.ascontiguousarray(v, dtype=np.float32) for k, v in m.items()}


def kernel(**inputs):
    nc = build_program()
    in_maps = [_slice_inputs(inputs, c) for c in range(NCORE)]
    r = run_bass_kernel_spmd(nc, in_maps, core_ids=list(range(NCORE)))
    outs = [np.asarray(r.results[c]["out"]) for c in range(NCORE)]
    full = np.concatenate([o.T for o in outs], axis=0)
    return full[None].astype(np.float32)
